# version 15
# baseline (speedup 1.0000x reference)
import numpy as np
import ml_dtypes
import concourse.bass as bass
import concourse.mybir as mybir
from concourse.bass_utils import run_bass_kernel_spmd

F32 = mybir.dt.float32
BF16 = mybir.dt.bfloat16
AF = mybir.ActivationFunctionType
ALU = mybir.AluOpType
AX = mybir.AxisListType

D = 2048
T = 8192
NCORES = 8
TPC = 1024
NFT = 48
NCC = 16
NQB = 16
INV_SQRT_HD = float(1.0 / np.sqrt(128.0))
EPS = 1e-5
NEGBIG = -1.0e30
QKR = 515
VR = 256
SHIFT = 60.0


def build_program():
    nc = bass.Bass("TRN2", target_bir_lowering=False)

    x_own = nc.dram_tensor("x_own", [TPC, D], F32, kind="ExternalInput").ap()
    w_attnT = nc.dram_tensor("w_attnT", [D, 3 * D], F32, kind="ExternalInput").ap()
    b_att_st = nc.dram_tensor("b_att_st", [128, NFT], F32, kind="ExternalInput").ap()
    w_projT = nc.dram_tensor("w_projT", [D, D], F32, kind="ExternalInput").ap()
    b_proj_in = nc.dram_tensor("b_proj_in", [1, D], F32, kind="ExternalInput").ap()
    ffn_w = nc.dram_tensor("ffn_w", [D, 136], F32, kind="ExternalInput").ap()
    dgnb = nc.dram_tensor("dgnb", [72, D], F32, kind="ExternalInput").ap()
    mean8 = nc.dram_tensor("mean8", [8, D], F32, kind="ExternalInput").ap()
    debias8 = nc.dram_tensor("debias8", [8, D], F32, kind="ExternalInput").ap()
    misc_in = nc.dram_tensor("misc_in", [1, 16], F32, kind="ExternalInput").ap()
    ident_in = nc.dram_tensor("ident_in", [128, 128], BF16, kind="ExternalInput").ap()
    causal_in = nc.dram_tensor("causal_in", [128, 128], F32, kind="ExternalInput").ap()
    out_own = nc.dram_tensor("out_own", [TPC, D], F32, kind="ExternalOutput").ap()

    a2a_qk = nc.dram_tensor("a2a_qk", [NCORES * QKR, TPC], BF16).ap()
    a2a_qko = nc.dram_tensor("a2a_qko", [NCORES * NCORES * QKR, TPC], BF16, addr_space="Shared").ap()
    a2a_v = nc.dram_tensor("a2a_v", [NCORES * VR, TPC], BF16).ap()
    a2a_vo = nc.dram_tensor("a2a_vo", [NCORES * NCORES * VR, TPC], BF16, addr_space="Shared").ap()
    y_in = nc.dram_tensor("y_in", [D, TPC], BF16).ap()
    y_out = nc.dram_tensor("y_out", [NCORES * D, TPC], BF16, addr_space="Shared").ap()
    my_qk = nc.dram_tensor("my_qk", [NCORES * QKR, TPC], BF16).ap()
    my_v = nc.dram_tensor("my_v", [NCORES * VR, TPC], BF16).ap()
    fence_in = nc.dram_tensor("fence_in", [1, 16], BF16).ap()
    fence_out = nc.dram_tensor("fence_out", [NCORES, 16], BF16, addr_space="Shared").ap()
    pid_holder = {}

    def SB(name, shape, dt):
        return nc.alloc_sbuf_tensor(name, shape, dt).ap()

    acc = SB("acc", [128, 8 * D], F32)
    reg_b = SB("reg_b", [128, 16384], BF16)
    reg_c = SB("reg_c", [128, 16640], BF16)
    reg_d = SB("reg_d", [128, 16384], BF16)
    reg_e = SB("reg_e", [128, 16384], BF16)

    ident_sb = SB("ident_sb", [128, 128], BF16)
    causal_sb = SB("causal_sb", [128, 128], F32)
    b_att_sb = SB("b_att_sb", [128, NFT], F32)
    ones_col_bf = SB("ones_col_bf", [128, 1], BF16)
    ones_col_f = SB("ones_col_f", [128, 1], F32)
    ones_row_f = SB("ones_row_f", [1, 128], F32)
    ones_row_bf = SB("ones_row_bf", [1, 128], BF16)
    qb_sb = SB("qb_sb", [1, 1024], BF16)
    misc_sb = SB("misc_sb", [1, 16], F32)
    rowscr = SB("rowscr", [1, 1024], F32)
    stats = SB("stats", [128, 16], F32)
    sq_ffn = SB("sq_ffn", [128, 8], F32)
    gmax_all = SB("gmax_all", [128, 8], F32)
    p_sb = SB("p_sb", [128, 1024], BF16)
    bc_sb = SB("bc_sb", [128, 512], F32)
    wp_buf = SB("wp_buf", [128, 1024], BF16)
    aux_bf = SB("aux_bf", [1, 64], BF16)

    xn_tm = reg_b
    K_sb = reg_b
    xn_fm = reg_c
    V_tm = reg_c
    Q_sb = reg_d
    y_sb = reg_e
    yown = reg_e

    xsq = reg_d[:, 0:4096].bitcast(F32)
    ktmp = reg_d[:, 0:2048].bitcast(F32)
    wt_buf = [reg_d[:, 4096:6144], reg_d[:, 6144:8192]]
    kqv_stage = [reg_d[:, 8192:9216], reg_d[:, 9216:10240]]
    out_stage = [reg_d[:, 10240:11264].bitcast(F32), reg_d[:, 11264:12288].bitcast(F32)]
    ffn_w_sb = reg_d[:, 12288:14464]
    dgnb_sb = reg_d[:, 4096:6144]

    A_sb = reg_e[:, 0:2448].bitcast(F32)
    umt = reg_e[:, 2448:2520]
    u_fm = reg_e[:, 2560:3584]
    ffn_tmp = reg_e[:, 3584:4608].bitcast(F32)
    bproj_bc = reg_e[:, 4608:8704].bitcast(F32)
    m8_f = reg_e[:, 8704:12800].bitcast(F32)
    m8_b = reg_e[:, 12800:14848]
    cross = reg_e[:, 14848:15392].bitcast(F32)
    c_bc = reg_e[:, 15392:15520].bitcast(F32)
    cu_bc = reg_e[:, 15520:15648].bitcast(F32)
    mm_bcv = reg_e[:, 15648:15664].bitcast(F32)
    cen_bc = reg_e[:, 15664:15680].bitcast(F32)
    slo_bc = reg_e[:, 15680:15696].bitcast(F32)

    RS_RINV = 0
    RS_MAXK = 512
    RS_AUXB = 544
    RS_AUXF = 608
    RS_NM = 624
    RS_T0 = 648
    RS_CROW = 704
    RS_CUROW = 768
    RS_MMS = 832
    RS_MM = 896
    RS_NTMP = 904

    def PSUM(name, shape):
        return nc.alloc_psum_tensor(name, shape, F32).ap()

    pA = PSUM("pA", [128, 1024])
    pB = PSUM("pB", [128, 1024])
    pC = PSUM("pC", [128, 1024])
    pD = PSUM("pD", [128, 512])
    pE = PSUM("pE", [128, 512])
    pD_bf = pD.bitcast(BF16)
    pE_bf = pE.bitcast(BF16)

    prog = {k: [] for k in ("sync", "gp", "t", "v", "s")}
    cnt = {}
    sems = {}
    for s in ("LD", "GP", "MM", "ACT", "VEC", "CC", "VS", "GS",
              "SCA", "SCB", "SCC", "LDC", "GPC", "GPA", "GPB"):
        sems[s] = nc.alloc_semaphore(s)
        cnt[s] = 0
    engmap = {}

    def inc(sem, n=1):
        cnt[sem] += n
        return (sem, cnt[sem], n)

    def I(eng, fn, waits=(), incs=()):
        waits = tuple(waits)
        incs = tuple(i for i in incs if i is not None)

        def run():
            e = engmap[eng]
            for s, v in waits:
                if v > 0:
                    e.wait_ge(sems[s], v)
            ins = fn()
            for s, _v, n in incs:
                ins = ins.then_inc(sems[s], n)
        prog[eng].append(run)

    def W(eng, *waits):
        waits = tuple(waits)

        def run():
            e = engmap[eng]
            for s, v in waits:
                if v > 0:
                    e.wait_ge(sems[s], v)
        prog[eng].append(run)

    def CK(eng, qsem, cksem):
        v = cnt[qsem]
        ic_ = inc(cksem)

        def run():
            e = engmap[eng]
            e.wait_ge(sems[qsem], v)
            e.nop().then_inc(sems[cksem], 1)
        prog[eng].append(run)
        return ic_[1]

    x_src = x_own.rearrange("(tt p) f -> p tt f", p=128)
    x_dst = acc.rearrange("p (tt f) -> p tt f", f=D)
    I("sync", lambda: nc.sync.dma_start(out=x_dst[:, 0:4, :], in_=x_src[:, 0:4, :]),
      incs=[inc("LD", 16)])
    for dst, src in ((ident_sb[:, :], ident_in), (causal_sb[:, :], causal_in),
                     (b_att_sb[:, :], b_att_st), (misc_sb[:, :], misc_in),
                     (m8_f[0:8, :], mean8)):
        I("sync", lambda dst=dst, src=src: nc.sync.dma_start(out=dst, in_=src),
          incs=[inc("LD", 16)])
    ck_ld1 = CK("sync", "LD", "LDC")
    I("sync", lambda: nc.sync.dma_start(out=x_dst[:, 4:8, :], in_=x_src[:, 4:8, :]),
      incs=[inc("LD", 16)])
    ck_xb = CK("sync", "LD", "LDC")

    for ap_ in (ones_col_bf[:, :], ones_col_f[:, :], ones_row_f[:, :], ones_row_bf[:, :]):
        I("gp", lambda ap_=ap_: nc.gpsimd.memset(ap_, 1.0))
    ig = inc("GS", 1)
    I("gp", lambda: nc.gpsimd.engine_nop(), incs=[ig])
    gs_ones = ig[1]

    ffn_w_src = ffn_w.rearrange("(cc p) f -> p cc f", p=128)
    ffn_w_dst = ffn_w_sb.rearrange("p (cc f) -> p cc f", f=136)
    I("gp", lambda: nc.gpsimd.dma_start(out=ffn_w_dst, in_=ffn_w_src), incs=[inc("GP", 16)])

    iv = inc("VS")
    I("v", lambda: nc.vector.tensor_scalar_mul(b_att_sb[:, 0:16], b_att_sb[:, 0:16], INV_SQRT_HD),
      waits=[("LDC", ck_ld1)], incs=[iv])
    vs_batt = iv[1]

    vec_ln = [0] * 8
    act_xsq = [0] * 8
    act_rt = [0] * 8
    vs_xsqred = [0] * 8
    for tt in range(8):
        a_t = acc[:, tt * D:(tt + 1) * D]
        ck_x = ck_ld1 if tt < 4 else ck_xb
        ia = inc("ACT")
        wts = [("LDC", ck_x)]
        if tt >= 1:
            wts.append(("VS", vs_xsqred[tt - 1]))
        I("s", lambda a_t=a_t: nc.scalar.activation(out=xsq[:, :], in_=a_t, func=AF.Square),
          waits=wts, incs=[ia])
        act_xsq[tt] = ia[1]

        I("v", lambda a_t=a_t, ck_x=ck_x: nc.vector.reduce_sum(stats[:, 0:1], a_t, axis=AX.X),
          waits=[("LDC", ck_x)])
        iv = inc("VS")
        I("v", lambda: nc.vector.reduce_sum(stats[:, 1:2], xsq[:, :], axis=AX.X),
          waits=[("ACT", act_xsq[tt])], incs=[iv])
        g0 = iv[1]
        vs_xsqred[tt] = g0
        I("v", lambda: nc.vector.tensor_scalar_mul(stats[:, 2:3], stats[:, 0:1], 1.0 / D),
          waits=[("VS", g0)])
        I("v", lambda: nc.vector.tensor_scalar_mul(stats[:, 3:4], stats[:, 1:2], 1.0 / D))
        iv = inc("VS")
        I("v", lambda: nc.vector.tensor_mul(stats[:, 4:5], stats[:, 2:3], stats[:, 2:3]), incs=[iv])
        g1 = iv[1]
        iv = inc("VS")
        I("v", lambda: nc.vector.tensor_sub(stats[:, 5:6], stats[:, 3:4], stats[:, 4:5]),
          waits=[("VS", g1)], incs=[iv])
        g2 = iv[1]
        iv = inc("VS")
        wts = [("VS", g2)]
        if tt >= 1:
            wts.append(("ACT", act_rt[tt - 1]))
        I("v", lambda: nc.vector.tensor_scalar_add(stats[:, 6:7], stats[:, 5:6], EPS),
          waits=wts, incs=[iv])
        g3 = iv[1]
        iv = inc("VS")
        I("v", lambda: nc.vector.reciprocal(stats[:, 7:8], stats[:, 6:7]),
          waits=[("VS", g3)], incs=[iv])
        g4 = iv[1]
        iv = inc("VS")
        I("v", lambda: nc.vector.tensor_mul(stats[:, 8:9], stats[:, 5:6], stats[:, 7:8]),
          waits=[("VS", g4)], incs=[iv])
        g5 = iv[1]
        I("v", lambda tt=tt: nc.vector.tensor_scalar_mul(sq_ffn[:, tt:tt + 1], stats[:, 8:9], float(D)),
          waits=[("VS", g5)])
        ia = inc("ACT")
        I("s", lambda: nc.scalar.activation(out=stats[:, 9:10], in_=stats[:, 6:7], func=AF.Sqrt),
          waits=[("VS", g3)], incs=[ia])
        act_rt[tt] = ia[1]
        iv = inc("VS")
        I("v", lambda: nc.vector.reciprocal(stats[:, 10:11], stats[:, 9:10]),
          waits=[("ACT", act_rt[tt])], incs=[iv])
        g6 = iv[1]
        I("v", lambda: nc.vector.tensor_scalar_add(stats[:, 11:12], stats[:, 10:11], 1.0),
          waits=[("VS", g6)])
        iv = inc("VS")
        I("v", lambda: nc.vector.tensor_mul(stats[:, 12:13], stats[:, 2:3], stats[:, 10:11]), incs=[iv])
        g7 = iv[1]
        I("v", lambda tt=tt, a_t=a_t: nc.vector.tensor_scalar(
            xn_tm[:, tt * D:(tt + 1) * D], a_t, stats[:, 2:3], stats[:, 10:11], ALU.subtract, ALU.mult),
          waits=[("VS", g7)])
        ivv = inc("VEC")
        I("v", lambda a_t=a_t: nc.vector.tensor_scalar(
            a_t, a_t, stats[:, 11:12], stats[:, 12:13], ALU.mult, ALU.subtract), incs=[ivv])
        vec_ln[tt] = ivv[1]
    vec_ln_all = cnt["VEC"]

    ev_tr = {}
    for tt in range(8):
        for half in range(2):
            pbuf = pD_bf if half == 0 else pE_bf
            idx = tt * 2 + half
            for j in range(8):
                cc = half * 8 + j
                wts = []
                if j == 0:
                    wts = [("VEC", vec_ln[tt])]
                    if idx == 0:
                        wts.append(("LDC", ck_ld1))
                    if idx >= 2:
                        wts.append(("VEC", ev_tr[idx - 2]))
                im = inc("MM") if j == 7 else None
                I("t", lambda tt=tt, cc=cc, j=j, pbuf=pbuf: nc.tensor.transpose(
                    pbuf[:, j * 128:(j + 1) * 128],
                    xn_tm[:, tt * D + cc * 128: tt * D + cc * 128 + 128],
                    ident_sb[:, :]),
                  waits=wts, incs=[im])
                if im:
                    mm_tr = im[1]
            ivv = inc("VEC")

            def evac_fm(tt=tt, half=half, pbuf=pbuf):
                src = pbuf.rearrange("p (j f) -> p j f", j=8)
                dst = xn_fm.rearrange("p (j r) -> p j r", r=1040)[
                    :, half * 8:(half + 1) * 8, tt * 128:tt * 128 + 128]
                return nc.vector.tensor_copy(dst, src)
            I("v", evac_fm, waits=[("MM", mm_tr)], incs=[ivv])
            ev_tr[idx] = ivv[1]
    vec_fm_all = cnt["VEC"]


    ck_wt = [0] * NFT
    ck_ffnw = 0
    mm_qkv = [0] * NFT
    act_stage = [0] * NFT
    sc_contrib = [0] * NFT
    mm_norm = {}
    vs_norm = {}
    ck_q2r = {}
    psum_qkv = [pA, pB]
    norm_fts = []
    SCP = ("SCA", "SCB")
    last_q2r_ck = 0

    for ft in range(NFT):
        par = ft % 2
        src = w_attnT[:, 128 * ft:128 * (ft + 1)].rearrange("(cc p) f -> p cc f", p=128)
        wts = []
        if ft >= 2:
            wts.append(("MM", mm_qkv[ft - 2]))
        I("gp", lambda src=src, par=par: nc.gpsimd.dma_start(
            out=wt_buf[par].rearrange("p (cc f) -> p cc f", f=128), in_=src),
          waits=wts, incs=[inc("GP", 16)])
        ck_wt[ft] = CK("gp", "GP", "GPC")
        if ft == 0:
            ck_ffnw = ck_wt[0]

        if ft < 16:
            sec, h = 0, ft
        elif ft < 32:
            sec, h = 1, ft - 16
        else:
            sec, h = 2, ft - 32
        ps = psum_qkv[par]
        wts0 = [("GPC", ck_wt[ft]), ("VEC", vec_fm_all)]
        if ft >= 2:
            wts0.append(("ACT", act_stage[ft - 2]))
        for cc in range(NCC):
            lhsT = wt_buf[par][:, cc * 128:(cc + 1) * 128]
            rhs0 = xn_fm[:, cc * 1040: cc * 1040 + 512]
            rhs1 = xn_fm[:, cc * 1040 + 512: cc * 1040 + 1024]
            I("t", lambda ps=ps, lhsT=lhsT, rhs0=rhs0, cc=cc: nc.tensor.matmul(
                ps[:, 0:512], lhsT, rhs0, start=(cc == 0), stop=(cc == NCC - 1)),
              waits=wts0 if cc == 0 else [])
            im = inc("MM") if cc == NCC - 1 else None
            I("t", lambda ps=ps, lhsT=lhsT, rhs1=rhs1, cc=cc: nc.tensor.matmul(
                ps[:, 512:1024], lhsT, rhs1, start=(cc == 0), stop=(cc == NCC - 1)),
              incs=[im])
        mm_qkv[ft] = cnt["MM"]

        scale = INV_SQRT_HD if sec == 0 else 1.0
        ia = inc("ACT")
        wts = [("MM", mm_qkv[ft]), ("VS", vs_batt)]
        if ft >= 2:
            wts.append((SCP[par], sc_contrib[ft - 2]))
        I("s", lambda ps=ps, par=par, ft=ft, scale=scale: nc.scalar.activation(
            out=kqv_stage[par][:, :], in_=ps[:, :], func=AF.Identity,
            bias=b_att_sb[:, ft:ft + 1], scale=scale),
          waits=wts, incs=[ia])
        act_stage[ft] = ia[1]

        if sec == 2:
            dst_t = a2a_v
            rowbase = VR * (h // 2) + (h % 2) * 128
        else:
            dst_t = a2a_qk
            rowbase = QKR * (h // 2) + sec * 256 + (h % 2) * 128
        ii = inc(SCP[par], 16)
        I("s", lambda par=par, dst_t=dst_t, rowbase=rowbase: nc.scalar.dma_start(
            out=dst_t[rowbase:rowbase + 128, :], in_=kqv_stage[par][:, :]),
          waits=[("ACT", act_stage[ft])], incs=[ii])
        sc_contrib[ft] = ii[1]

        if sec in (0, 1):
            nidx = len(norm_fts)
            ia = inc("ACT")
            wts = []
            if norm_fts:
                wts.append(("MM", mm_norm[norm_fts[-1]]))
            if ft == 0:
                wts.append(("VEC", vec_ln[6]))
            I("s", lambda par=par: nc.scalar.activation(
                out=ktmp[:, 0:1024], in_=kqv_stage[par][:, :], func=AF.Square),
              waits=wts, incs=[ia])
            act_kt = ia[1]
            wts = [("ACT", act_kt)]
            if norm_fts:
                wts.append(("VS", vs_norm[norm_fts[-1]]))
            else:
                wts.append(("GS", gs_ones))
            I("t", lambda: nc.tensor.matmul(pC[0:1, 0:512], ones_col_f[:, :], ktmp[:, 0:512],
                                            start=True, stop=True), waits=wts)
            im = inc("MM")
            I("t", lambda: nc.tensor.matmul(pC[0:1, 512:1024], ones_col_f[:, :], ktmp[:, 512:1024],
                                            start=True, stop=True), incs=[im])
            mm_norm[ft] = im[1]
            if sec == 0:
                iv2 = inc("VS")
                wts = [("MM", mm_norm[ft])]
                if last_q2r_ck:
                    wts.append(("LDC", last_q2r_ck))
                I("v", lambda: nc.vector.tensor_copy(qb_sb[0:1, :], pC[0:1, 0:1024]),
                  waits=wts, incs=[iv2])
                vs_norm[ft] = iv2[1]
                rowq = QKR * (h // 2) + 512 + (h % 2)
                I("sync", lambda rowq=rowq: nc.sync.dma_start(
                    out=a2a_qk[rowq:rowq + 1, :], in_=qb_sb[0:1, :]),
                  waits=[("VS", iv2[1])], incs=[inc("LD", 16)])
                ck_q2r[ft] = CK("sync", "LD", "LDC")
                last_q2r_ck = ck_q2r[ft]
            else:
                s0 = RS_NTMP + 2 * nidx
                iv = inc("VS")
                I("v", lambda s0=s0: nc.vector.reduce_max(
                    rowscr[0:1, s0:s0 + 1], pC[0:1, 0:512], axis=AX.X),
                  waits=[("MM", mm_norm[ft])], incs=[iv])
                iv2 = inc("VS")
                I("v", lambda s0=s0: nc.vector.reduce_max(
                    rowscr[0:1, s0 + 1:s0 + 2], pC[0:1, 512:1024], axis=AX.X),
                  incs=[iv2])
                vs_norm[ft] = iv2[1]
            norm_fts.append(ft)
        if ft == 31:
            W("s", ("SCA", sc_contrib[30]), ("SCB", sc_contrib[31]))
            isc = inc("SCC")
            I("s", lambda: nc.scalar.nop(), incs=[isc])
            scc_qk = isc[1]
    act_qkv_all = cnt["ACT"]
    vs_norm_all = vs_norm[norm_fts[-1]]
    W("s", ("SCA", cnt["SCA"]), ("SCB", cnt["SCB"]))
    isc = inc("SCC")
    I("s", lambda: nc.scalar.nop(), incs=[isc])
    scc_contrib = isc[1]

    ivv = None
    kfts = [ft for ft in norm_fts if ft >= 16]
    for i, ft in enumerate(kfts):
        nidx = norm_fts.index(ft)
        h = ft - 16
        slot = RS_MAXK + h
        s0 = RS_NTMP + 2 * nidx
        ivv = inc("VEC") if i == len(kfts) - 1 else None
        I("v", lambda slot=slot, s0=s0: nc.vector.tensor_max(
            rowscr[0:1, slot:slot + 1], rowscr[0:1, s0:s0 + 1], rowscr[0:1, s0 + 1:s0 + 2]),
          waits=[("VS", vs_norm_all)] if i == 0 else [], incs=[ivv])
    vec_norm_all = ivv[1]

    ivv = inc("VEC")
    I("v", lambda: nc.vector.tensor_copy(
        aux_bf[0:1, 0:16], rowscr[0:1, RS_MAXK:RS_MAXK + 16]),
      waits=[("VEC", vec_norm_all)], incs=[ivv])
    vec_aux = ivv[1]
    for j in range(NCORES):
        row = QKR * j + 514
        I("sync", lambda j=j, row=row: nc.sync.dma_start(
            out=a2a_qk[row:row + 1, 0:2], in_=aux_bf[0:1, 2 * j:2 * j + 2],
            allow_slow_non_contiguous=True),
          waits=[("VEC", vec_aux)] if j == 0 else [], incs=[inc("LD", 16)])
    ck_aux = CK("sync", "LD", "LDC")
    ic = inc("CC")
    I("gp", lambda: nc.gpsimd.collective_compute(
        "AllGather", ALU.bypass, replica_groups=[list(range(NCORES))],
        ins=[a2a_qk[:, :]], outs=[a2a_qko[:, :]]),
      waits=[("LDC", ck_aux), ("SCC", scc_qk)], incs=[ic])
    cc_qk_raw = ic[1]
    ic = inc("CC")
    I("gp", lambda: nc.gpsimd.collective_compute(
        "AllGather", ALU.bypass, replica_groups=[list(range(NCORES))],
        ins=[fence_in[:, :]], outs=[fence_out[:, :]]),
      waits=[("CC", cc_qk_raw)], incs=[ic])
    cc_qk = ic[1]

    iv = inc("VS")
    I("v", lambda: nc.vector.tensor_copy(m8_b[0:8, :], m8_f[0:8, :]),
      waits=[("LDC", ck_ld1)], incs=[iv])
    vs_m8b = iv[1]
    iv = inc("VS")
    I("v", lambda: nc.vector.tensor_mul(m8_f[0:8, :], m8_f[0:8, :], m8_f[0:8, :]),
      waits=[("VS", vs_m8b)], incs=[iv])
    vs_msq = iv[1]
    ivv = inc("VEC")
    I("v", lambda: nc.vector.reduce_sum(stats[0:8, 13:14], m8_f[0:8, :], axis=AX.X),
      waits=[("VS", vs_msq)], incs=[ivv])
    vec_mm = ivv[1]
    for n in range(8):
        I("sync", lambda n=n: nc.sync.dma_start(out=rowscr[0:1, RS_MMS + 8 * n:RS_MMS + 8 * n + 1],
                                                in_=stats[n:n + 1, 13:14],
                                                allow_slow_non_contiguous=True),
          waits=[("VEC", vec_mm)] if n == 0 else [], incs=[inc("LD", 16)])
    ck_mmrow = CK("sync", "LD", "LDC")
    iv = inc("VS")
    I("v", lambda: nc.vector.tensor_copy(
        rowscr[0:1, RS_MM:RS_MM + 8],
        rowscr[0:1, RS_MMS:RS_MMS + 64].rearrange("a (j q) -> a j q", q=8)[:, :, 0:1].squeeze(2)),
      waits=[("LDC", ck_mmrow)], incs=[iv])
    vs_mmrow = iv[1]

    vec_mtr = {}
    for half in range(2):
        for j in range(8):
            cc = half * 8 + j
            wts = []
            if cc == 0:
                wts = [("VS", vs_m8b), ("VEC", vec_fm_all)]
            elif j == 0:
                wts = [("VEC", vec_mtr[0])]
            im = inc("MM") if j == 7 else None
            I("t", lambda cc=cc, j=j: nc.tensor.transpose(
                pD_bf[:, j * 8:(j + 1) * 8],
                m8_b[0:8, cc * 128:(cc + 1) * 128], ident_sb[0:8, 0:8]),
              waits=wts, incs=[im])
            if im:
                mm_m = im[1]
        ivv = inc("VEC")

        def evac_m(half=half):
            src = pD_bf[:, 0:64].rearrange("p (j f) -> p j f", j=8)
            dst = xn_fm.rearrange("p (j r) -> p j r", r=1040)[
                :, half * 8:(half + 1) * 8, 1024:1032]
            return nc.vector.tensor_copy(dst, src)
        I("v", evac_m, waits=[("MM", mm_m)], incs=[ivv])
        vec_mtr[half] = ivv[1]

    I("sync", lambda: nc.sync.dma_start(out=m8_f[0:8, :], in_=debias8),
      waits=[("VEC", vec_mm)], incs=[inc("LD", 16)])
    ck_deb = CK("sync", "LD", "LDC")
    iv = inc("VS")
    I("v", lambda: nc.vector.tensor_copy(m8_b[0:8, :], m8_f[0:8, :]),
      waits=[("LDC", ck_deb), ("VEC", vec_mtr[1])], incs=[iv])
    vs_debb = iv[1]
    vec_dtr = {}
    for half in range(2):
        for j in range(8):
            cc = half * 8 + j
            wts = []
            if cc == 0:
                wts = [("VS", vs_debb)]
            elif j == 0:
                wts = [("VEC", vec_dtr[0])]
            im = inc("MM") if j == 7 else None
            I("t", lambda cc=cc, j=j: nc.tensor.transpose(
                pE_bf[:, j * 8:(j + 1) * 8],
                m8_b[0:8, cc * 128:(cc + 1) * 128], ident_sb[0:8, 0:8]),
              waits=wts, incs=[im])
            if im:
                mm_d = im[1]
        ivv = inc("VEC")

        def evac_d(half=half):
            src = pE_bf[:, 0:64].rearrange("p (j f) -> p j f", j=8)
            dst = xn_fm.rearrange("p (j r) -> p j r", r=1040)[
                :, half * 8:(half + 1) * 8, 1032:1040]
            return nc.vector.tensor_copy(dst, src)
        I("v", evac_d, waits=[("MM", mm_d)], incs=[ivv])
        vec_dtr[half] = ivv[1]
    vec_xm_all = cnt["VEC"]


    I("gp", lambda: nc.gpsimd.dma_start(out=dgnb_sb[0:72, :], in_=dgnb),
      waits=[("MM", mm_qkv[NFT - 2])], incs=[inc("GP", 16)])
    ck_dgnb = CK("gp", "GP", "GPC")

    mm_ffnA = [0] * 9
    vec_ffnA = [0] * 9
    for tile in range(9):
        par = tile % 2
        ps = psum_qkv[par]
        ncols = 128 if tile < 8 else 16
        wts0 = [("GPC", ck_ffnw), ("ACT", act_qkv_all)]
        if tile == 0:
            wts0.append(("VEC", vec_xm_all))
        if tile >= 2:
            wts0.append(("VEC", vec_ffnA[tile - 2]))
        for cc in range(NCC):
            colbase = cc * 1040 + tile * 128 if tile < 8 else cc * 1040 + 1024
            im = inc("MM") if cc == NCC - 1 else None
            I("t", lambda ps=ps, cc=cc, ncols=ncols, colbase=colbase: nc.tensor.matmul(
                ps[0:ncols, 0:136],
                xn_fm[:, colbase: colbase + ncols],
                ffn_w_sb[:, cc * 136:(cc + 1) * 136],
                start=(cc == 0), stop=(cc == NCC - 1)),
              waits=wts0 if cc == 0 else [], incs=[im])
        mm_ffnA[tile] = cnt["MM"]
        ivv = inc("VEC")
        if tile < 8:
            I("v", lambda ps=ps, tile=tile: nc.vector.tensor_copy(
                A_sb[:, tile * 136:(tile + 1) * 136], ps[0:128, 0:136]),
              waits=[("MM", mm_ffnA[tile])], incs=[ivv])
        else:
            I("v", lambda ps=ps: nc.vector.tensor_copy(cross[0:16, 0:136], ps[0:16, 0:136]),
              waits=[("MM", mm_ffnA[tile])], incs=[ivv])
        vec_ffnA[tile] = ivv[1]
    vec_cross = vec_ffnA[8]

    for n in range(8):
        I("gp", lambda n=n: nc.gpsimd.dma_start(
            out=rowscr[0:1, RS_CROW + 8 * n: RS_CROW + 8 * n + 8],
            in_=cross[n:n + 1, 8 * n:8 * n + 8], allow_slow_non_contiguous=True),
          waits=[("VEC", vec_cross)] if n == 0 else [], incs=[inc("GP", 16)])
        I("gp", lambda n=n: nc.gpsimd.dma_start(
            out=rowscr[0:1, RS_CUROW + 8 * n: RS_CUROW + 8 * n + 8],
            in_=cross[8 + n:9 + n, 72 + 8 * n:72 + 8 * n + 8], allow_slow_non_contiguous=True),
          incs=[inc("GP", 16)])
    ck_rows = CK("gp", "GP", "GPC")

    ic = inc("CC")
    I("gp", lambda: nc.gpsimd.collective_compute(
        "AllGather", ALU.bypass, replica_groups=[list(range(NCORES))],
        ins=[a2a_v[:, :]], outs=[a2a_vo[:, :]]),
      waits=[("SCC", scc_contrib)], incs=[ic])
    cc_v_raw = ic[1]
    ic = inc("CC")
    I("gp", lambda: nc.gpsimd.collective_compute(
        "AllGather", ALU.bypass, replica_groups=[list(range(NCORES))],
        ins=[fence_in[:, :]], outs=[fence_out[:, :]]),
      waits=[("CC", cc_v_raw)], incs=[ic])
    cc_v = ic[1]

    bclist = [
        (c_bc[:, :], rowscr[0:1, RS_CROW:RS_CROW + 64], 64, [("GPC", ck_rows)]),
        (cu_bc[:, :], rowscr[0:1, RS_CUROW:RS_CUROW + 64], 64, []),
        (mm_bcv[:, :], rowscr[0:1, RS_MM:RS_MM + 8], 8, [("VS", vs_mmrow)]),
        (cen_bc[:, :], misc_sb[0:1, 0:8], 8, [("LDC", ck_ld1)]),
        (slo_bc[:, :], misc_sb[0:1, 8:16], 8, []),
    ]
    vec_bcs = {}
    for i, (dst, srcrow, n, extra) in enumerate(bclist):
        off = 0 if i % 2 == 0 else 512
        wts = list(extra) + [("GS", gs_ones), ("VEC", vec_norm_all)]
        if i >= 2:
            wts.append(("VEC", vec_bcs[i - 2]))
        im = inc("MM")
        I("t", lambda srcrow=srcrow, n=n, off=off: nc.tensor.matmul(
            pC[:, off:off + n], ones_row_f[:, :], srcrow, start=True, stop=True),
          waits=wts, incs=[im])
        ivv = inc("VEC")
        I("v", lambda dst=dst, n=n, off=off: nc.vector.tensor_copy(dst, pC[:, off:off + n]),
          waits=[("MM", im[1])], incs=[ivv])
        vec_bcs[i] = ivv[1]
    prev_bp_mm = 0
    for g in range(4):
        wts = [] if g == 0 else [("MM", prev_bp_mm)]
        I("sync", lambda g=g: nc.sync.dma_start(out=rowscr[0:1, 0:512],
                                                in_=b_proj_in[0:1, 512 * g:512 * (g + 1)]),
          waits=wts, incs=[inc("LD", 16)])
        ck_bp = CK("sync", "LD", "LDC")
        im = inc("MM")
        wts = [("LDC", ck_bp), ("GS", gs_ones)]
        wts.append(("VEC", vec_bcs[4] if g == 0 else vec_bcs[len(bclist) + g - 1]))
        I("t", lambda: nc.tensor.matmul(pC[:, 0:512], ones_row_f[:, :], rowscr[0:1, 0:512],
                                        start=True, stop=True),
          waits=wts, incs=[im])
        prev_bp_mm = im[1]
        ivv = inc("VEC")
        I("v", lambda g=g: nc.vector.tensor_copy(bproj_bc[:, 512 * g:512 * (g + 1)], pC[:, 0:512]),
          waits=[("MM", im[1])], incs=[ivv])
        vec_bcs[len(bclist) + g] = ivv[1]
    vec_bc_all = cnt["VEC"]

    vec_ffn_done = [0] * 8
    vec_utr_prev = 0
    for tt in range(8):
        At = A_sb[:, tt * 136:(tt + 1) * 136]
        iv = inc("VS")
        I("v", lambda At=At: nc.vector.tensor_sub(ffn_tmp[:, 0:64], At[:, 0:64], c_bc[:, :]),
          waits=[("VEC", vec_bc_all), ("VEC", vec_ffnA[tt])], incs=[iv])
        g0 = iv[1]
        iv = inc("VS")
        I("v", lambda: nc.vector.tensor_mul(ffn_tmp[:, 64:128], ffn_tmp[:, 0:64], ffn_tmp[:, 0:64]),
          waits=[("VS", g0)], incs=[iv])
        g1 = iv[1]
        for n in range(8):
            iv = inc("VS") if n == 7 else None
            I("v", lambda n=n: nc.vector.reduce_sum(
                ffn_tmp[:, 128 + n:129 + n], ffn_tmp[:, 64 + 8 * n:64 + 8 * n + 8], axis=AX.X),
              waits=[("VS", g1)] if n == 0 else [], incs=[iv])
        g2 = cnt["VS"]
        iv = inc("VS")
        I("v", lambda At=At, tt=tt: nc.vector.tensor_scalar(
            ffn_tmp[:, 136:144], At[:, 64:72], -2.0, sq_ffn[:, tt:tt + 1], ALU.mult, ALU.add),
          waits=[("VS", g2)], incs=[iv])
        g3 = iv[1]
        iv = inc("VS")
        I("v", lambda: nc.vector.tensor_add(ffn_tmp[:, 144:152], ffn_tmp[:, 136:144], mm_bcv[:, :]),
          waits=[("VS", g3)], incs=[iv])
        g4 = iv[1]
        iv = inc("VS")
        I("v", lambda: nc.vector.reciprocal(ffn_tmp[:, 152:160], ffn_tmp[:, 144:152]),
          waits=[("VS", g4)], incs=[iv])
        g5 = iv[1]
        iv = inc("VS")
        I("v", lambda: nc.vector.tensor_mul(ffn_tmp[:, 160:168], ffn_tmp[:, 128:136], ffn_tmp[:, 152:160]),
          waits=[("VS", g5)], incs=[iv])
        g6 = iv[1]
        iv = inc("VS")
        I("v", lambda: nc.vector.tensor_sub(ffn_tmp[:, 168:176], ffn_tmp[:, 160:168], cen_bc[:, :]),
          waits=[("VS", g6)], incs=[iv])
        g7 = iv[1]
        iv = inc("VS")
        I("v", lambda: nc.vector.tensor_mul(ffn_tmp[:, 176:184], ffn_tmp[:, 168:176], slo_bc[:, :]),
          waits=[("VS", g7)], incs=[iv])
        g8 = iv[1]
        ia = inc("ACT")
        I("s", lambda: nc.scalar.activation(out=ffn_tmp[:, 184:192], in_=ffn_tmp[:, 176:184], func=AF.Sigmoid),
          waits=[("VS", g8)], incs=[ia])
        act_gate = ia[1]
        iv = inc("VS")
        I("v", lambda tt=tt: nc.vector.reduce_max(gmax_all[:, tt:tt + 1], ffn_tmp[:, 184:192], axis=AX.X),
          waits=[("ACT", act_gate)], incs=[iv])
        g9 = iv[1]
        iv = inc("VS")
        I("v", lambda tt=tt: nc.vector.tensor_scalar(
            ffn_tmp[:, 192:200], ffn_tmp[:, 184:192], gmax_all[:, tt:tt + 1], None, ALU.is_ge),
          waits=[("VS", g9)], incs=[iv])
        g10 = iv[1]
        iv = inc("VS")
        I("v", lambda: nc.vector.tensor_scalar_mul(ffn_tmp[:, 200:201], ffn_tmp[:, 192:193], 0.0),
          waits=[("VS", g10)], incs=[iv])
        gpx = iv[1]
        for n in range(1, 8):
            iv = inc("VS")
            I("v", lambda n=n: nc.vector.tensor_add(
                ffn_tmp[:, 200 + n:201 + n], ffn_tmp[:, 199 + n:200 + n], ffn_tmp[:, 191 + n:192 + n]),
              waits=[("VS", gpx)], incs=[iv])
            gpx = iv[1]
        g11 = cnt["VS"]
        iv = inc("VS")
        I("v", lambda: nc.vector.tensor_scalar(
            ffn_tmp[:, 208:216], ffn_tmp[:, 200:208], 0.0, None, ALU.is_equal),
          waits=[("VS", g11)], incs=[iv])
        g12 = iv[1]
        iv = inc("VS")
        I("v", lambda: nc.vector.tensor_mul(ffn_tmp[:, 216:224], ffn_tmp[:, 192:200], ffn_tmp[:, 208:216]),
          waits=[("VS", g12)], incs=[iv])
        g13 = iv[1]
        iv = inc("VS")
        I("v", lambda At=At: nc.vector.tensor_sub(ffn_tmp[:, 0:64], At[:, 72:136], cu_bc[:, :]),
          waits=[("VS", g13)], incs=[iv])
        g14 = iv[1]
        for n in range(8):
            iv = inc("VS") if n == 7 else None
            I("v", lambda n=n: nc.vector.tensor_scalar(
                umt[:, 8 * n:8 * n + 8], ffn_tmp[:, 8 * n:8 * n + 8],
                ffn_tmp[:, 216 + n:217 + n], None, ALU.mult),
              waits=[("VS", g14)] if n == 0 else [], incs=[iv])
        g15 = cnt["VS"]
        ivv = inc("VEC")
        I("v", lambda: nc.vector.tensor_copy(umt[:, 64:72], ffn_tmp[:, 216:224]),
          waits=[("VS", g15)], incs=[ivv])
        vec_umt = ivv[1]
        im = inc("MM")
        wts = [("VEC", vec_umt)]
        wts.append(("VEC", vec_utr_prev if tt >= 1 else vec_mtr[1]))
        I("t", lambda: nc.tensor.transpose(pD_bf[0:72, 0:128], umt[:, 0:72], ident_sb[:, :]),
          waits=wts, incs=[im])
        ivv = inc("VEC")
        I("v", lambda tt=tt: nc.vector.tensor_copy(u_fm[0:72, tt * 128:(tt + 1) * 128], pD_bf[0:72, 0:128]),
          waits=[("MM", im[1])], incs=[ivv])
        vec_utr_prev = ivv[1]
        mm_mod = {}
        for fg in range(4):
            ps = [pA[:, 0:512], pA[:, 512:1024], pB[:, 0:512], pB[:, 512:1024]][fg]
            wts = [("VEC", vec_utr_prev), ("GPC", ck_dgnb)]
            wts.append(("VEC", vec_ffn_done[tt - 1] if tt >= 1 else vec_ffnA[8]))
            im = inc("MM")
            I("t", lambda ps=ps, tt=tt, fg=fg: nc.tensor.matmul(
                ps, u_fm[0:72, tt * 128:(tt + 1) * 128],
                dgnb_sb[0:72, 512 * fg:512 * (fg + 1)], start=True, stop=True),
              waits=wts, incs=[im])
            mm_mod[fg] = im[1]
        for fg in range(4):
            ps = [pA[:, 0:512], pA[:, 512:1024], pB[:, 0:512], pB[:, 512:1024]][fg]
            xnsl = xn_tm[:, tt * D + 512 * fg: tt * D + 512 * (fg + 1)]
            asl = acc[:, tt * D + 512 * fg: tt * D + 512 * (fg + 1)]
            iv = inc("VS")
            I("v", lambda ps=ps, xnsl=xnsl: nc.vector.tensor_sub(ffn_tmp[:, 0:512], ps, xnsl),
              waits=[("MM", mm_mod[fg])], incs=[iv])
            ivv = inc("VEC")
            I("v", lambda asl=asl, tt=tt: nc.vector.scalar_tensor_tensor(
                asl, ffn_tmp[:, 0:512], gmax_all[:, tt:tt + 1], asl, ALU.mult, ALU.add),
              waits=[("VS", iv[1])], incs=[ivv])
        ivv = inc("VEC")
        I("v", lambda tt=tt: nc.vector.tensor_add(
            acc[:, tt * D:(tt + 1) * D], acc[:, tt * D:(tt + 1) * D], bproj_bc[:, :]),
          incs=[ivv])
        vec_ffn_done[tt] = ivv[1]
    vec_ffn_all = cnt["VEC"]
    vec_utr_last = vec_utr_prev

    def pid_init():
        pid_holder["sync"] = nc.partition_id(engines=[mybir.EngineType.SP])
        return nc.sync.nop()
    I("sync", pid_init)

    def qk_slab():
        pid = pid_holder["sync"]
        v = a2a_qko.rearrange("(i s r) t -> i s r t", s=NCORES, r=QKR)
        src_ = v[0:NCORES, bass.ts(pid, 1), :, :].squeeze(1)
        dst_ = my_qk.rearrange("(i r) t -> i r t", r=QKR)
        return nc.sync.dma_start(out=dst_, in_=src_)
    I("sync", qk_slab, waits=[("CC", cc_qk)], incs=[inc("LD", 16)])
    ck_slab = CK("sync", "LD", "LDC")

    a2v = my_qk.rearrange("(i r) t -> i r t", r=QKR)
    vvv = my_v.rearrange("(i r) t -> i r t", r=VR)

    def a2v_my(i0, i1, r0, r1):
        return a2v[i0:i1, r0:r1, :].unsqueeze(1)

    def vv_my(i0, i1, r0, r1):
        return vvv[i0:i1, r0:r1, :].unsqueeze(1)

    for j in range(NCORES):
        I("sync", lambda j=j: nc.sync.dma_start(
            out=rowscr[0:1, RS_AUXB + 8 * j:RS_AUXB + 8 * j + 1].bitcast(BF16),
            in_=a2v_my(j, j + 1, 514, 515)[:, :, :, 0:2].squeeze(2).squeeze(1),
            allow_slow_non_contiguous=True),
          incs=[inc("LD", 16)])
    ck_auxout = CK("sync", "LD", "LDC")
    iv = inc("VS")
    I("v", lambda: nc.vector.tensor_copy(
        rowscr[0:1, RS_AUXF:RS_AUXF + 16].rearrange("a (j h) -> a j h", h=2),
        rowscr[0:1, RS_AUXB:RS_AUXB + 64].bitcast(BF16).rearrange(
            "a (j q) -> a j q", q=16)[:, :, 0:2]),
      waits=[("LDC", ck_auxout)], incs=[iv])
    g0 = iv[1]
    AF_ = RS_AUXF
    PM = RS_T0
    iv = inc("VS")
    I("v", lambda: nc.vector.tensor_copy(rowscr[0:1, PM:PM + 2], rowscr[0:1, AF_:AF_ + 2]),
      waits=[("VS", g0)], incs=[iv])
    gp_ = iv[1]
    for j in range(1, NCORES):
        iv = inc("VS")
        I("v", lambda j=j: nc.vector.tensor_max(
            rowscr[0:1, PM + 2 * j:PM + 2 * j + 2],
            rowscr[0:1, PM + 2 * (j - 1):PM + 2 * j],
            rowscr[0:1, AF_ + 2 * j:AF_ + 2 * j + 2]),
          waits=[("VS", gp_)], incs=[iv])
        gp_ = iv[1]
    ia = inc("ACT")
    I("s", lambda: nc.scalar.activation(out=rowscr[0:1, RS_T0 + 16:RS_T0 + 32],
                                        in_=rowscr[0:1, PM:PM + 16], func=AF.Sqrt),
      waits=[("VS", gp_)], incs=[ia])
    act_m = ia[1]
    ivv = inc("VEC")
    I("v", lambda: nc.vector.tensor_scalar_mul(rowscr[0:1, RS_NM:RS_NM + 16],
                                               rowscr[0:1, RS_T0 + 16:RS_T0 + 32], -1.0),
      waits=[("ACT", act_m)], incs=[ivv])
    vec_negm = ivv[1]

    for hl in range(2):
        wts = [("VEC", vec_ffn_all), ("SCC", scc_contrib)] if hl == 0 else []
        I("sync", lambda hl=hl: nc.sync.dma_start(
            out=K_sb.rearrange("p (h j t) -> p h j t", h=2, j=8)[:, hl:hl + 1, :, :],
            in_=a2v_my(0, NCORES, 256 + hl * 128, 256 + (hl + 1) * 128).squeeze(1)
                .transpose([1, 0, 2]).unsqueeze(1)),
          waits=wts, incs=[inc("LD", 16)])
        I("sync", lambda hl=hl: nc.sync.dma_start(
            out=Q_sb.rearrange("p (h j t) -> p h j t", h=2, j=8)[:, hl:hl + 1, :, :],
            in_=a2v_my(0, NCORES, hl * 128, (hl + 1) * 128).squeeze(1)
                .transpose([1, 0, 2]).unsqueeze(1)),
          incs=[inc("LD", 16)])
    ck_kq = CK("sync", "LD", "LDC")

    def v_slab():
        pid = pid_holder["sync"]
        v = a2a_vo.rearrange("(i s r) t -> i s r t", s=NCORES, r=VR)
        src_ = v[0:NCORES, bass.ts(pid, 1), :, :].squeeze(1)
        dst_ = my_v.rearrange("(i r) t -> i r t", r=VR)
        return nc.sync.dma_start(out=dst_, in_=src_)
    I("sync", v_slab, waits=[("CC", cc_v)], incs=[inc("LD", 16)])
    ck_vslab = CK("sync", "LD", "LDC")

    vstage = [p_sb[:, 0:1024], bc_sb.bitcast(BF16)[:, 0:1024]]
    ck_vst = {}
    mm_vtr = {}
    vec_vtr = {}
    for hl in range(2):
        for j in range(NCORES):
            k = hl * 8 + j
            par = k % 2
            wts = []
            if k >= 2:
                wts.append(("MM", mm_vtr[k - 2]))
            I("sync", lambda hl=hl, j=j, par=par: nc.sync.dma_start(
                out=vstage[par][:, :],
                in_=vv_my(j, j + 1, hl * 128, (hl + 1) * 128).squeeze(1).squeeze(0)),
              waits=wts, incs=[inc("LD", 16)])
            ck_vst[k] = CK("sync", "LD", "LDC")
            pbuf = pD_bf if par == 0 else pE_bf
            for ttv in range(8):
                wts = []
                if ttv == 0:
                    wts = [("LDC", ck_vst[k])]
                    if k == 0:
                        wts.append(("VEC", vec_utr_last))
                    elif k == 1:
                        wts.append(("VEC", vec_xm_all))
                    else:
                        wts.append(("VEC", vec_vtr[k - 2]))
                im = inc("MM") if ttv == 7 else None
                I("t", lambda par=par, ttv=ttv, pbuf=pbuf: nc.tensor.transpose(
                    pbuf[:, ttv * 128:(ttv + 1) * 128], vstage[par][:, ttv * 128:(ttv + 1) * 128],
                    ident_sb[:, :]),
                  waits=wts, incs=[im])
            mm_vtr[k] = cnt["MM"]
            ivv = inc("VEC")
            I("v", lambda hl=hl, j=j, pbuf=pbuf: nc.vector.tensor_copy(
                V_tm[:, hl * 8192 + j * 1024: hl * 8192 + (j + 1) * 1024], pbuf[:, 0:1024]),
              waits=[("MM", mm_vtr[k])], incs=[ivv])
            vec_vtr[k] = ivv[1]
    vec_vtr_all = cnt["VEC"]

    def q2p_dma(m, waits=()):
        hl_, b_ = m // NQB, m % NQB
        rowq = QKR * (b_ // 2) + 512 + hl_
        c0 = (b_ % 2) * 512
        ii = inc(SCP[m % 2], 16)
        I("s", lambda rowq=rowq, c0=c0: nc.scalar.dma_start(
            out=qb_sb[0:1, 0:512], in_=my_qk[rowq:rowq + 1, c0:c0 + 512]),
          waits=list(waits), incs=[ii])
        return ii[1]

    sp = [pA[:, 0:512], pA[:, 512:1024], pD[:, 0:512]]
    obank = [pB[:, 0:512], pC[:, 0:512]]
    sbank = [pB[0:1, 512:1024], pC[0:1, 512:1024]]
    bbc_sb = wp_buf.bitcast(F32)
    mm_kq_of = {}
    vec_badd_of = {}
    act_exp_of = {}
    mm_cons_of = {}
    vec_yev = {}
    vec_bcev = 0
    vec_bbc = 0
    pd_guard = 0
    chunk_ctr = 0
    prev_blk_mm = 0
    sc_q2p = {0: q2p_dma(0, waits=[("LDC", ck_slab)])}
    for hl in range(2):
        for b in range(NQB):
            m = hl * NQB + b
            pbk = b % 2
            nchunks = 4 * b + 4
            ia = inc("ACT")
            wts = [(SCP[m % 2], sc_q2p[m])]
            if prev_blk_mm:
                wts.append(("MM", prev_blk_mm))
            I("s", lambda: nc.scalar.activation(out=qb_sb[0:1, 512:1024], in_=qb_sb[0:1, 0:512],
                                                func=AF.Sqrt),
              waits=wts, incs=[ia])
            act_sq = ia[1]
            if m + 1 < 2 * NQB:
                sc_q2p[m + 1] = q2p_dma(m + 1, waits=[("ACT", act_sq)])
            iv = inc("VS")
            nmcol = RS_NM + 2 * (b // 2) + hl
            wts = [("ACT", act_sq)]
            if m == 0:
                wts.append(("VEC", vec_negm))
            I("v", lambda nmcol=nmcol: nc.vector.tensor_scalar(
                qb_sb[0:1, 512:1024], qb_sb[0:1, 512:1024],
                rowscr[0:1, nmcol:nmcol + 1], SHIFT, ALU.mult, ALU.add),
              waits=wts, incs=[iv])
            vs_B = iv[1]
            im = inc("MM")
            wts = [("VS", vs_B)]
            if m == 0:
                wts += [("LDC", ck_kq), ("VEC", vec_vtr_all)]
            else:
                wts.append(("VEC", vec_bbc))
            I("t", lambda: nc.tensor.matmul(pE[:, 0:512], ones_row_bf[:, :],
                                            qb_sb[0:1, 512:1024], start=True, stop=True),
              waits=wts, incs=[im])
            mm_bbc = im[1]
            ivv = inc("VEC")
            I("v", lambda: nc.vector.tensor_copy(bbc_sb[:, :], pE[:, 0:512]),
              waits=[("MM", mm_bbc)], incs=[ivv])
            vec_bbc = ivv[1]
            def emit_pv(gi, kcl, hl=hl, b=b, pbk=pbk, m=m, nchunks=nchunks):
                j0 = max(0, 128 * kcl - 512 * b)
                first = (kcl == 0)
                last = (kcl == nchunks - 1)
                p2 = gi % 2
                wts = [("ACT", act_exp_of[gi])]
                if first and m >= 2:
                    wts.append(("VEC", vec_yev[m - 2]))
                I("t", lambda: nc.tensor.matmul(
                    obank[pbk][:, j0:512],
                    V_tm[:, hl * 8192 + kcl * 128: hl * 8192 + (kcl + 1) * 128],
                    p_sb[:, p2 * 512 + j0: p2 * 512 + 512],
                    start=first, stop=last), waits=wts)
                im = inc("MM")
                I("t", lambda: nc.tensor.matmul(
                    sbank[pbk][:, j0:512], ones_col_bf[:, :],
                    p_sb[:, p2 * 512 + j0: p2 * 512 + 512],
                    start=first, stop=last), incs=[im])
                mm_cons_of[gi] = im[1]

            g0blk = chunk_ctr
            for kc in range(nchunks):
                gi = chunk_ctr
                chunk_ctr += 1
                s3 = gi % 3
                p2 = gi % 2
                j0 = max(0, 128 * kc - 512 * b)
                diag = (kc >= 4 * b)
                wts = []
                if gi >= 3:
                    wts.append(("ACT", act_exp_of[gi - 3]))
                else:
                    wts.append(("VEC", vec_vtr_all))
                if s3 == 2 and pd_guard:
                    wts.append(("VEC", pd_guard))
                    pd_guard = 0
                im = inc("MM")
                I("t", lambda hl=hl, b=b, kc=kc, s3=s3, j0=j0: nc.tensor.matmul(
                    sp[s3][:, j0:512],
                    K_sb[:, hl * 8192 + kc * 128: hl * 8192 + (kc + 1) * 128],
                    Q_sb[:, hl * 8192 + 512 * b + j0: hl * 8192 + 512 * (b + 1)],
                    start=True, stop=True), waits=wts, incs=[im])
                mm_kq_of[gi] = im[1]
                ivv = inc("VEC")
                wts = [("MM", mm_kq_of[gi])]
                if kc == 0:
                    wts.append(("VEC", vec_bbc))
                I("v", lambda s3=s3, j0=j0: nc.vector.tensor_add(
                    sp[s3][:, j0:512], sp[s3][:, j0:512], bbc_sb[:, j0:512]),
                  waits=wts, incs=[ivv])
                if diag:
                    ivv = inc("VEC")
                    I("v", lambda s3=s3, j0=j0: nc.vector.tensor_add(
                        sp[s3][:, j0:j0 + 128], sp[s3][:, j0:j0 + 128], causal_sb[:, :]),
                      incs=[ivv])
                vec_badd_of[gi] = ivv[1]
                if kc >= 2:
                    emit_pv(g0blk + kc - 2, kc - 2)
                wts = [("VEC", vec_badd_of[gi])]
                if gi >= 2:
                    wts.append(("MM", mm_cons_of[gi - 2]))
                ia = inc("ACT")
                I("s", lambda s3=s3, p2=p2, j0=j0: nc.scalar.activation(
                    out=p_sb[:, p2 * 512 + j0: p2 * 512 + 512], in_=sp[s3][:, j0:512],
                    func=AF.Exp),
                  waits=wts, incs=[ia])
                act_exp_of[gi] = ia[1]
            emit_pv(g0blk + nchunks - 2, nchunks - 2)
            emit_pv(g0blk + nchunks - 1, nchunks - 1)
            mm_last = mm_cons_of[chunk_ctr - 1]
            prev_blk_mm = mm_last
            iv = inc("VS")
            I("v", lambda pbk=pbk: nc.vector.reciprocal(rowscr[0:1, 0:512], sbank[pbk]),
              waits=[("MM", mm_last)], incs=[iv])
            g_r = iv[1]
            im = inc("MM")
            wts = [("VS", g_r), ("ACT", act_exp_of[chunk_ctr - 1])]
            if vec_bcev:
                wts.append(("VEC", vec_bcev))
            I("t", lambda: nc.tensor.matmul(pD[:, 0:512], ones_row_f[:, :], rowscr[0:1, 0:512],
                                            start=True, stop=True), waits=wts, incs=[im])
            ivv = inc("VEC")
            I("v", lambda: nc.vector.tensor_copy(bc_sb[:, :], pD[:, 0:512]),
              waits=[("MM", im[1])], incs=[ivv])
            vec_bcev = ivv[1]
            pd_guard = vec_bcev
            ivv = inc("VEC")
            I("v", lambda hl=hl, b=b, pbk=pbk: nc.vector.tensor_mul(
                y_sb[:, hl * 8192 + 512 * b: hl * 8192 + 512 * (b + 1)], obank[pbk], bc_sb[:, :]),
              waits=[("VEC", vec_bcev)], incs=[ivv])
            vec_yev[m] = ivv[1]
    vec_y_all = cnt["VEC"]
    act_exp_all = act_exp_of[chunk_ctr - 1]

    for hl in range(2):
        I("sync", lambda hl=hl: nc.sync.dma_start(
            out=y_in.rearrange("(j l p) t -> p l j t", l=2, p=128)[:, hl:hl + 1, :, :],
            in_=y_sb.rearrange("p (h j t) -> p h j t", h=2, j=8)[:, hl:hl + 1, :, :]),
          waits=[("VEC", vec_y_all)] if hl == 0 else [], incs=[inc("LD", 16)])
    ck_y = CK("sync", "LD", "LDC")
    ic = inc("CC")
    I("gp", lambda: nc.gpsimd.collective_compute(
        "AllGather", ALU.bypass, replica_groups=[list(range(NCORES))],
        ins=[y_in[:, :]], outs=[y_out[:, :]]),
      waits=[("LDC", ck_y)], incs=[ic])
    ic = inc("CC")
    I("gp", lambda: nc.gpsimd.collective_compute(
        "AllGather", ALU.bypass, replica_groups=[list(range(NCORES))],
        ins=[fence_in[:, :]], outs=[fence_out[:, :]]),
      waits=[("CC", ic[1] - 1)], incs=[ic])
    cc_y = ic[1]

    ck_yown = [0] * 16

    def yown_src(fi):
        pid = pid_holder["sync"]
        v = y_out.rearrange("(i s l p) t -> i s l p t", s=NCORES, l=2, p=128)
        return v[fi // 2:fi // 2 + 1, bass.ts(pid, 1), fi % 2:fi % 2 + 1, :, :] \
            .squeeze(2).squeeze(1).squeeze(0)

    for fi in range(16):
        I("sync", lambda fi=fi: nc.sync.dma_start(
            out=yown[:, fi * 1024:(fi + 1) * 1024], in_=yown_src(fi)),
          waits=[("CC", cc_y)] if fi == 0 else [], incs=[inc("LD", 16)])
        ck_yown[fi] = CK("sync", "LD", "LDC")

    qbanks = [pA[:, 0:512], pA[:, 512:1024], pB[:, 0:512], pB[:, 512:1024],
              pC[:, 0:512], pC[:, 512:1024], pD[:, 0:512], pE[:, 0:512]]
    mm_wp_cons = {}
    vec_oev = {}
    ck_out = {}
    ck_wp_of = {}
    GPP = ("GPA", "GPB")
    gpc0 = cnt["GPC"]
    for g in range(4):
        for fi in range(16):
            k = g * 16 + fi
            par = k % 2
            wts = []
            if k >= 2:
                wts.append(("MM", mm_wp_cons[k - 2]))
            I("gp", lambda g=g, fi=fi, par=par: nc.gpsimd.dma_start(
                out=wp_buf[:, par * 512:(par + 1) * 512],
                in_=w_projT[fi * 128:(fi + 1) * 128, g * 512:(g + 1) * 512]),
              waits=wts, incs=[inc(GPP[par], 16)])
            if k >= 1:
                ck_wp_of[k - 1] = CK("gp", GPP[(k - 1) % 2], "GPC")
            if k == 63:
                ck_wp_of[63] = CK("gp", GPP[1], "GPC")
            for qt in range(8):
                wts = []
                if qt == 0:
                    wts += [("GPC", gpc0 + k + 1), ("LDC", ck_yown[fi])]
                if fi == 0:
                    if g == 0:
                        if qt == 0:
                            wts += [("VEC", vec_y_all), ("ACT", act_exp_all), ("VEC", vec_bcev)]
                    else:
                        wts.append(("VEC", vec_oev[(g - 1, qt)]))
                im = inc("MM") if qt == 7 else None
                I("t", lambda fi=fi, qt=qt, par=par: nc.tensor.matmul(
                    qbanks[qt], yown[:, fi * 1024 + qt * 128: fi * 1024 + (qt + 1) * 128],
                    wp_buf[:, par * 512:(par + 1) * 512],
                    start=(fi == 0), stop=(fi == 15)),
                  waits=wts, incs=[im])
            mm_wp_cons[k] = cnt["MM"]
        for qt in range(8):
            k = g * 8 + qt
            par = k % 2
            wts = [("MM", mm_wp_cons[g * 16 + 15])]
            if k >= 2:
                wts.append(("LDC", ck_out[k - 2]))
            ivv = inc("VEC")
            I("v", lambda g=g, qt=qt, par=par: nc.vector.tensor_add(
                out_stage[par][:, 0:512], qbanks[qt],
                acc[:, qt * D + 512 * g: qt * D + 512 * (g + 1)]),
              waits=wts, incs=[ivv])
            vec_oev[(g, qt)] = ivv[1]
            I("sync", lambda g=g, qt=qt, par=par: nc.sync.dma_start(
                out=out_own[qt * 128:(qt + 1) * 128, g * 512:(g + 1) * 512],
                in_=out_stage[par][:, 0:512]),
              waits=[("VEC", vec_oev[(g, qt)])], incs=[inc("LD", 16)])
            ck_out[k] = CK("sync", "LD", "LDC")
    W("sync", ("LD", cnt["LD"]))

    with nc.Block() as block:
        @block.sync
        def _(eng):
            engmap["sync"] = nc.sync
            for fn in prog["sync"]:
                fn()

        @block.gpsimd
        def _(eng):
            engmap["gp"] = nc.gpsimd
            for fn in prog["gp"]:
                fn()

        @block.tensor
        def _(eng):
            engmap["t"] = nc.tensor
            for fn in prog["t"]:
                fn()

        @block.vector
        def _(eng):
            engmap["v"] = nc.vector
            for fn in prog["v"]:
                fn()

        @block.scalar
        def _(eng):
            engmap["s"] = nc.scalar
            for fn in prog["s"]:
                fn()

    return nc


_NC_CACHE = [None]


def kernel(L0_attn_x, L0_attn_w_attn, L0_attn_b_attn, L0_attn_w_proj, L0_attn_b_proj,
           L0_ffn_select_weight, L0_ffn_select_mean_diff, L0_ffn_imp_center,
           L0_ffn_imp_slope, L0_ffn_lora_update, L0_ffn_lora_degen,
           L0_ffn_bias_w, L0_ffn_debias_w):
    bf = ml_dtypes.bfloat16
    x2d = np.ascontiguousarray(np.asarray(L0_attn_x, dtype=np.float32).reshape(T, D))
    w_attnT = np.ascontiguousarray(np.asarray(L0_attn_w_attn, dtype=np.float32).T)
    b_att_st = np.ascontiguousarray(np.asarray(L0_attn_b_attn, dtype=np.float32).reshape(NFT, 128).T)
    w_projT = np.ascontiguousarray(np.asarray(L0_attn_w_proj, dtype=np.float32).T)
    b_proj_in = np.ascontiguousarray(np.asarray(L0_attn_b_proj, dtype=np.float32).reshape(1, D))
    sw = np.asarray(L0_ffn_select_weight, dtype=np.float32)
    md = np.asarray(L0_ffn_select_mean_diff, dtype=np.float32)
    lu = np.asarray(L0_ffn_lora_update, dtype=np.float32)
    ldg = np.asarray(L0_ffn_lora_degen, dtype=np.float32)
    bw = np.asarray(L0_ffn_bias_w, dtype=np.float32)
    dw = np.asarray(L0_ffn_debias_w, dtype=np.float32)
    ffn_w = np.ascontiguousarray(np.concatenate(
        [sw.transpose(1, 0, 2).reshape(D, 64), md.T, lu.transpose(1, 0, 2).reshape(D, 64)], axis=1))
    dgnb = np.ascontiguousarray(np.concatenate([ldg.transpose(0, 2, 1).reshape(64, D), bw], axis=0))
    misc = np.ascontiguousarray(np.concatenate(
        [np.asarray(L0_ffn_imp_center, dtype=np.float32),
         np.asarray(L0_ffn_imp_slope, dtype=np.float32)]).reshape(1, 16))
    ident = np.eye(128, dtype=bf)
    kk, jj = np.meshgrid(np.arange(128), np.arange(128), indexing="ij")
    causal = np.where(jj >= kk, 0.0, NEGBIG).astype(np.float32)

    if _NC_CACHE[0] is None:
        _NC_CACHE[0] = build_program()
    nc = _NC_CACHE[0]

    in_maps = []
    for c in range(NCORES):
        in_maps.append({
            "x_own": np.ascontiguousarray(x2d[c * TPC:(c + 1) * TPC]),
            "w_attnT": w_attnT,
            "b_att_st": b_att_st,
            "w_projT": w_projT,
            "b_proj_in": b_proj_in,
            "ffn_w": ffn_w,
            "dgnb": dgnb,
            "mean8": md,
            "debias8": dw,
            "misc_in": misc,
            "ident_in": ident,
            "causal_in": causal,
        })
    res = run_bass_kernel_spmd(nc, in_maps, core_ids=list(range(NCORES)))
    out = np.concatenate([res.results[c]["out_own"] for c in range(NCORES)], axis=0)
    return out.reshape(1, T, D)


# version 17 (final)
# speedup vs baseline: 1.0135x; 1.0135x over previous
import numpy as np
import ml_dtypes
import concourse.bass as bass
import concourse.mybir as mybir
from concourse.bass_utils import run_bass_kernel_spmd

F32 = mybir.dt.float32
BF16 = mybir.dt.bfloat16
AF = mybir.ActivationFunctionType
ALU = mybir.AluOpType
AX = mybir.AxisListType

D = 2048
T = 8192
NCORES = 8
TPC = 1024
NFT = 48
NCC = 16
NQB = 16
INV_SQRT_HD = float(1.0 / np.sqrt(128.0))
EPS = 1e-5
NEGBIG = -1.0e30
SHROWS = 771
SHIFT = 60.0
A2A_ROWS = SHROWS * NCORES


def build_program():
    nc = bass.Bass("TRN2", target_bir_lowering=False)

    x_own = nc.dram_tensor("x_own", [TPC, D], F32, kind="ExternalInput").ap()
    w_attnT = nc.dram_tensor("w_attnT", [D, 3 * D], F32, kind="ExternalInput").ap()
    b_att_st = nc.dram_tensor("b_att_st", [128, NFT], F32, kind="ExternalInput").ap()
    w_projT = nc.dram_tensor("w_projT", [D, D], F32, kind="ExternalInput").ap()
    b_proj_in = nc.dram_tensor("b_proj_in", [1, D], F32, kind="ExternalInput").ap()
    ffn_w = nc.dram_tensor("ffn_w", [D, 136], F32, kind="ExternalInput").ap()
    dgnb = nc.dram_tensor("dgnb", [72, D], F32, kind="ExternalInput").ap()
    mean8 = nc.dram_tensor("mean8", [8, D], F32, kind="ExternalInput").ap()
    debias8 = nc.dram_tensor("debias8", [8, D], F32, kind="ExternalInput").ap()
    misc_in = nc.dram_tensor("misc_in", [1, 16], F32, kind="ExternalInput").ap()
    ident_in = nc.dram_tensor("ident_in", [128, 128], BF16, kind="ExternalInput").ap()
    causal_in = nc.dram_tensor("causal_in", [128, 128], F32, kind="ExternalInput").ap()
    out_own = nc.dram_tensor("out_own", [TPC, D], F32, kind="ExternalOutput").ap()

    a2a_in = nc.dram_tensor("a2a_in", [A2A_ROWS, TPC], BF16).ap()
    a2a_out = nc.dram_tensor("a2a_out", [NCORES * A2A_ROWS, TPC], BF16, addr_space="Shared").ap()
    y_in = nc.dram_tensor("y_in", [D, TPC], BF16).ap()
    y_out = nc.dram_tensor("y_out", [NCORES * D, TPC], BF16, addr_space="Shared").ap()
    my_qkv = nc.dram_tensor("my_qkv", [NCORES * SHROWS, TPC], BF16).ap()
    fence_in = nc.dram_tensor("fence_in", [1, 16], BF16).ap()
    fence_out = nc.dram_tensor("fence_out", [NCORES, 16], BF16, addr_space="Shared").ap()
    pid_holder = {}

    def SB(name, shape, dt):
        return nc.alloc_sbuf_tensor(name, shape, dt).ap()

    acc = SB("acc", [128, 8 * D], F32)
    reg_b = SB("reg_b", [128, 16384], BF16)
    reg_c = SB("reg_c", [128, 16640], BF16)
    reg_d = SB("reg_d", [128, 16384], BF16)
    reg_e = SB("reg_e", [128, 16384], BF16)

    ident_sb = SB("ident_sb", [128, 128], BF16)
    causal_sb = SB("causal_sb", [128, 128], F32)
    b_att_sb = SB("b_att_sb", [128, NFT], F32)
    ones_col_bf = SB("ones_col_bf", [128, 1], BF16)
    ones_col_f = SB("ones_col_f", [128, 1], F32)
    ones_row_f = SB("ones_row_f", [1, 128], F32)
    ones_row_bf = SB("ones_row_bf", [1, 128], BF16)
    qb_sb = SB("qb_sb", [1, 1024], BF16)
    misc_sb = SB("misc_sb", [1, 16], F32)
    rowscr = SB("rowscr", [1, 1024], F32)
    stats = SB("stats", [128, 16], F32)
    sq_ffn = SB("sq_ffn", [128, 8], F32)
    gmax_all = SB("gmax_all", [128, 8], F32)
    p_sb = SB("p_sb", [128, 1024], BF16)
    bc_sb = SB("bc_sb", [128, 512], F32)
    wp_buf = SB("wp_buf", [128, 1024], BF16)
    aux_bf = SB("aux_bf", [1, 64], BF16)

    xn_tm = reg_b
    K_sb = reg_b
    xn_fm = reg_c
    V_tm = reg_c
    Q_sb = reg_d
    y_sb = reg_e
    yown = reg_e

    xsq = reg_d[:, 0:4096].bitcast(F32)
    ktmp = reg_d[:, 0:2048].bitcast(F32)
    wt_buf = [reg_d[:, 4096:6144], reg_d[:, 6144:8192]]
    kqv_stage = [reg_d[:, 8192:9216], reg_d[:, 9216:10240]]
    out_stage = [reg_d[:, 10240:11264].bitcast(F32), reg_d[:, 11264:12288].bitcast(F32)]
    ffn_w_sb = reg_d[:, 12288:14464]
    dgnb_sb = reg_d[:, 4096:6144]

    A_sb = reg_e[:, 0:2448].bitcast(F32)
    umt = reg_e[:, 2448:2520]
    u_fm = reg_e[:, 2560:3584]
    ffn_tmp = reg_e[:, 3584:4608].bitcast(F32)
    bproj_bc = reg_e[:, 4608:8704].bitcast(F32)
    m8_f = reg_e[:, 8704:12800].bitcast(F32)
    m8_b = reg_e[:, 12800:14848]
    cross = reg_e[:, 14848:15392].bitcast(F32)
    c_bc = reg_e[:, 15392:15520].bitcast(F32)
    cu_bc = reg_e[:, 15520:15648].bitcast(F32)
    mm_bcv = reg_e[:, 15648:15664].bitcast(F32)
    cen_bc = reg_e[:, 15664:15680].bitcast(F32)
    slo_bc = reg_e[:, 15680:15696].bitcast(F32)

    RS_RINV = 0
    RS_MAXK = 512
    RS_AUXB = 544
    RS_AUXF = 608
    RS_NM = 624
    RS_T0 = 648
    RS_CROW = 704
    RS_CUROW = 768
    RS_MMS = 832
    RS_MM = 896
    RS_NTMP = 904

    def PSUM(name, shape):
        return nc.alloc_psum_tensor(name, shape, F32).ap()

    pA = PSUM("pA", [128, 1024])
    pB = PSUM("pB", [128, 1024])
    pC = PSUM("pC", [128, 1024])
    pD = PSUM("pD", [128, 512])
    pE = PSUM("pE", [128, 512])
    pD_bf = pD.bitcast(BF16)
    pE_bf = pE.bitcast(BF16)

    prog = {k: [] for k in ("sync", "gp", "t", "v", "s")}
    cnt = {}
    sems = {}
    for s in ("LD", "GP", "MM", "ACT", "VEC", "CC", "VS", "GS",
              "SCA", "SCB", "SCC", "LDC", "GPC", "GPA", "GPB"):
        sems[s] = nc.alloc_semaphore(s)
        cnt[s] = 0
    engmap = {}

    def inc(sem, n=1):
        cnt[sem] += n
        return (sem, cnt[sem], n)

    def I(eng, fn, waits=(), incs=()):
        waits = tuple(waits)
        incs = tuple(i for i in incs if i is not None)

        def run():
            e = engmap[eng]
            for s, v in waits:
                if v > 0:
                    e.wait_ge(sems[s], v)
            ins = fn()
            for s, _v, n in incs:
                ins = ins.then_inc(sems[s], n)
        prog[eng].append(run)

    def W(eng, *waits):
        waits = tuple(waits)

        def run():
            e = engmap[eng]
            for s, v in waits:
                if v > 0:
                    e.wait_ge(sems[s], v)
        prog[eng].append(run)

    def CK(eng, qsem, cksem):
        v = cnt[qsem]
        ic_ = inc(cksem)

        def run():
            e = engmap[eng]
            e.wait_ge(sems[qsem], v)
            e.nop().then_inc(sems[cksem], 1)
        prog[eng].append(run)
        return ic_[1]

    x_src = x_own.rearrange("(tt p) f -> p tt f", p=128)
    x_dst = acc.rearrange("p (tt f) -> p tt f", f=D)
    I("sync", lambda: nc.sync.dma_start(out=x_dst[:, 0:4, :], in_=x_src[:, 0:4, :]),
      incs=[inc("LD", 16)])
    for dst, src in ((ident_sb[:, :], ident_in), (causal_sb[:, :], causal_in),
                     (b_att_sb[:, :], b_att_st), (misc_sb[:, :], misc_in),
                     (m8_f[0:8, :], mean8)):
        I("sync", lambda dst=dst, src=src: nc.sync.dma_start(out=dst, in_=src),
          incs=[inc("LD", 16)])
    ck_ld1 = CK("sync", "LD", "LDC")
    I("sync", lambda: nc.sync.dma_start(out=x_dst[:, 4:8, :], in_=x_src[:, 4:8, :]),
      incs=[inc("LD", 16)])
    ck_xb = CK("sync", "LD", "LDC")

    for ap_ in (ones_col_bf[:, :], ones_col_f[:, :], ones_row_f[:, :], ones_row_bf[:, :]):
        I("gp", lambda ap_=ap_: nc.gpsimd.memset(ap_, 1.0))
    ig = inc("GS", 1)
    I("gp", lambda: nc.gpsimd.engine_nop(), incs=[ig])
    gs_ones = ig[1]

    ffn_w_src = ffn_w.rearrange("(cc p) f -> p cc f", p=128)
    ffn_w_dst = ffn_w_sb.rearrange("p (cc f) -> p cc f", f=136)
    I("gp", lambda: nc.gpsimd.dma_start(out=ffn_w_dst, in_=ffn_w_src), incs=[inc("GP", 16)])

    iv = inc("VS")
    I("v", lambda: nc.vector.tensor_scalar_mul(b_att_sb[:, 0:16], b_att_sb[:, 0:16], INV_SQRT_HD),
      waits=[("LDC", ck_ld1)], incs=[iv])
    vs_batt = iv[1]

    vec_ln = [0] * 8
    act_xsq = [0] * 8
    act_rt = [0] * 8
    vs_xsqred = [0] * 8
    for tt in range(8):
        a_t = acc[:, tt * D:(tt + 1) * D]
        ck_x = ck_ld1 if tt < 4 else ck_xb
        ia = inc("ACT")
        wts = [("LDC", ck_x)]
        if tt >= 1:
            wts.append(("VS", vs_xsqred[tt - 1]))
        I("s", lambda a_t=a_t: nc.scalar.activation(out=xsq[:, :], in_=a_t, func=AF.Square),
          waits=wts, incs=[ia])
        act_xsq[tt] = ia[1]

        I("v", lambda a_t=a_t, ck_x=ck_x: nc.vector.reduce_sum(stats[:, 0:1], a_t, axis=AX.X),
          waits=[("LDC", ck_x)])
        iv = inc("VS")
        I("v", lambda: nc.vector.reduce_sum(stats[:, 1:2], xsq[:, :], axis=AX.X),
          waits=[("ACT", act_xsq[tt])], incs=[iv])
        g0 = iv[1]
        vs_xsqred[tt] = g0
        I("v", lambda: nc.vector.tensor_scalar_mul(stats[:, 2:3], stats[:, 0:1], 1.0 / D),
          waits=[("VS", g0)])
        I("v", lambda: nc.vector.tensor_scalar_mul(stats[:, 3:4], stats[:, 1:2], 1.0 / D))
        iv = inc("VS")
        I("v", lambda: nc.vector.tensor_mul(stats[:, 4:5], stats[:, 2:3], stats[:, 2:3]), incs=[iv])
        g1 = iv[1]
        iv = inc("VS")
        I("v", lambda: nc.vector.tensor_sub(stats[:, 5:6], stats[:, 3:4], stats[:, 4:5]),
          waits=[("VS", g1)], incs=[iv])
        g2 = iv[1]
        iv = inc("VS")
        wts = [("VS", g2)]
        if tt >= 1:
            wts.append(("ACT", act_rt[tt - 1]))
        I("v", lambda: nc.vector.tensor_scalar_add(stats[:, 6:7], stats[:, 5:6], EPS),
          waits=wts, incs=[iv])
        g3 = iv[1]
        iv = inc("VS")
        I("v", lambda: nc.vector.reciprocal(stats[:, 7:8], stats[:, 6:7]),
          waits=[("VS", g3)], incs=[iv])
        g4 = iv[1]
        iv = inc("VS")
        I("v", lambda: nc.vector.tensor_mul(stats[:, 8:9], stats[:, 5:6], stats[:, 7:8]),
          waits=[("VS", g4)], incs=[iv])
        g5 = iv[1]
        I("v", lambda tt=tt: nc.vector.tensor_scalar_mul(sq_ffn[:, tt:tt + 1], stats[:, 8:9], float(D)),
          waits=[("VS", g5)])
        ia = inc("ACT")
        I("s", lambda: nc.scalar.activation(out=stats[:, 9:10], in_=stats[:, 6:7], func=AF.Sqrt),
          waits=[("VS", g3)], incs=[ia])
        act_rt[tt] = ia[1]
        iv = inc("VS")
        I("v", lambda: nc.vector.reciprocal(stats[:, 10:11], stats[:, 9:10]),
          waits=[("ACT", act_rt[tt])], incs=[iv])
        g6 = iv[1]
        I("v", lambda: nc.vector.tensor_scalar_add(stats[:, 11:12], stats[:, 10:11], 1.0),
          waits=[("VS", g6)])
        iv = inc("VS")
        I("v", lambda: nc.vector.tensor_mul(stats[:, 12:13], stats[:, 2:3], stats[:, 10:11]), incs=[iv])
        g7 = iv[1]
        I("v", lambda tt=tt, a_t=a_t: nc.vector.tensor_scalar(
            xn_tm[:, tt * D:(tt + 1) * D], a_t, stats[:, 2:3], stats[:, 10:11], ALU.subtract, ALU.mult),
          waits=[("VS", g7)])
        ivv = inc("VEC")
        I("v", lambda a_t=a_t: nc.vector.tensor_scalar(
            a_t, a_t, stats[:, 11:12], stats[:, 12:13], ALU.mult, ALU.subtract), incs=[ivv])
        vec_ln[tt] = ivv[1]
    vec_ln_all = cnt["VEC"]

    ev_tr = {}
    for tt in range(8):
        for half in range(2):
            pbuf = pD_bf if half == 0 else pE_bf
            idx = tt * 2 + half
            for j in range(8):
                cc = half * 8 + j
                wts = []
                if j == 0:
                    wts = [("VEC", vec_ln[tt])]
                    if idx == 0:
                        wts.append(("LDC", ck_ld1))
                    if idx >= 2:
                        wts.append(("VEC", ev_tr[idx - 2]))
                im = inc("MM") if j == 7 else None
                I("t", lambda tt=tt, cc=cc, j=j, pbuf=pbuf: nc.tensor.transpose(
                    pbuf[:, j * 128:(j + 1) * 128],
                    xn_tm[:, tt * D + cc * 128: tt * D + cc * 128 + 128],
                    ident_sb[:, :]),
                  waits=wts, incs=[im])
                if im:
                    mm_tr = im[1]
            ivv = inc("VEC")

            def evac_fm(tt=tt, half=half, pbuf=pbuf):
                src = pbuf.rearrange("p (j f) -> p j f", j=8)
                dst = xn_fm.rearrange("p (j r) -> p j r", r=1040)[
                    :, half * 8:(half + 1) * 8, tt * 128:tt * 128 + 128]
                return nc.vector.tensor_copy(dst, src)
            I("v", evac_fm, waits=[("MM", mm_tr)], incs=[ivv])
            ev_tr[idx] = ivv[1]
    vec_fm_all = cnt["VEC"]


    ck_wt = [0] * NFT
    ck_ffnw = 0
    mm_qkv = [0] * NFT
    act_stage = [0] * NFT
    sc_contrib = [0] * NFT
    mm_norm = {}
    vs_norm = {}
    ck_q2r = {}
    psum_qkv = [pA, pB]
    norm_fts = []
    SCP = ("SCA", "SCB")
    last_q2r_ck = 0

    for ft in range(NFT):
        par = ft % 2
        src = w_attnT[:, 128 * ft:128 * (ft + 1)].rearrange("(cc p) f -> p cc f", p=128)
        wts = []
        if ft >= 2:
            wts.append(("MM", mm_qkv[ft - 2]))
        I("gp", lambda src=src, par=par: nc.gpsimd.dma_start(
            out=wt_buf[par].rearrange("p (cc f) -> p cc f", f=128), in_=src),
          waits=wts, incs=[inc("GP", 16)])
        ck_wt[ft] = CK("gp", "GP", "GPC")
        if ft == 0:
            ck_ffnw = ck_wt[0]

        if ft < 16:
            sec, h = 0, ft
        elif ft < 32:
            sec, h = 1, ft - 16
        else:
            sec, h = 2, ft - 32
        ps = psum_qkv[par]
        wts0 = [("GPC", ck_wt[ft]), ("VEC", vec_fm_all)]
        if ft >= 2:
            wts0.append(("ACT", act_stage[ft - 2]))
        for cc in range(NCC):
            lhsT = wt_buf[par][:, cc * 128:(cc + 1) * 128]
            rhs0 = xn_fm[:, cc * 1040: cc * 1040 + 512]
            rhs1 = xn_fm[:, cc * 1040 + 512: cc * 1040 + 1024]
            I("t", lambda ps=ps, lhsT=lhsT, rhs0=rhs0, cc=cc: nc.tensor.matmul(
                ps[:, 0:512], lhsT, rhs0, start=(cc == 0), stop=(cc == NCC - 1)),
              waits=wts0 if cc == 0 else [])
            im = inc("MM") if cc == NCC - 1 else None
            I("t", lambda ps=ps, lhsT=lhsT, rhs1=rhs1, cc=cc: nc.tensor.matmul(
                ps[:, 512:1024], lhsT, rhs1, start=(cc == 0), stop=(cc == NCC - 1)),
              incs=[im])
        mm_qkv[ft] = cnt["MM"]

        scale = INV_SQRT_HD if sec == 0 else 1.0
        ia = inc("ACT")
        wts = [("MM", mm_qkv[ft]), ("VS", vs_batt)]
        if ft >= 2:
            wts.append((SCP[par], sc_contrib[ft - 2]))
        I("s", lambda ps=ps, par=par, ft=ft, scale=scale: nc.scalar.activation(
            out=kqv_stage[par][:, :], in_=ps[:, :], func=AF.Identity,
            bias=b_att_sb[:, ft:ft + 1], scale=scale),
          waits=wts, incs=[ia])
        act_stage[ft] = ia[1]

        rowbase = SHROWS * (h // 2) + sec * 256 + (h % 2) * 128
        ii = inc(SCP[par], 16)
        I("s", lambda par=par, rowbase=rowbase: nc.scalar.dma_start(
            out=a2a_in[rowbase:rowbase + 128, :], in_=kqv_stage[par][:, :]),
          waits=[("ACT", act_stage[ft])], incs=[ii])
        sc_contrib[ft] = ii[1]

        if sec in (0, 1):
            nidx = len(norm_fts)
            ia = inc("ACT")
            wts = []
            if norm_fts:
                wts.append(("MM", mm_norm[norm_fts[-1]]))
            if ft == 0:
                wts.append(("VEC", vec_ln[6]))
            I("s", lambda par=par: nc.scalar.activation(
                out=ktmp[:, 0:1024], in_=kqv_stage[par][:, :], func=AF.Square),
              waits=wts, incs=[ia])
            act_kt = ia[1]
            wts = [("ACT", act_kt)]
            if norm_fts:
                wts.append(("VS", vs_norm[norm_fts[-1]]))
            else:
                wts.append(("GS", gs_ones))
            I("t", lambda: nc.tensor.matmul(pC[0:1, 0:512], ones_col_f[:, :], ktmp[:, 0:512],
                                            start=True, stop=True), waits=wts)
            im = inc("MM")
            I("t", lambda: nc.tensor.matmul(pC[0:1, 512:1024], ones_col_f[:, :], ktmp[:, 512:1024],
                                            start=True, stop=True), incs=[im])
            mm_norm[ft] = im[1]
            if sec == 0:
                iv2 = inc("VS")
                wts = [("MM", mm_norm[ft])]
                if last_q2r_ck:
                    wts.append(("LDC", last_q2r_ck))
                I("v", lambda: nc.vector.tensor_copy(qb_sb[0:1, :], pC[0:1, 0:1024]),
                  waits=wts, incs=[iv2])
                vs_norm[ft] = iv2[1]
                ia2 = inc("ACT")
                I("s", lambda: nc.scalar.activation(out=qb_sb[0:1, :], in_=qb_sb[0:1, :],
                                                    func=AF.Sqrt),
                  waits=[("VS", iv2[1])], incs=[ia2])
                rowq = SHROWS * (h // 2) + 768 + (h % 2)
                I("sync", lambda rowq=rowq: nc.sync.dma_start(
                    out=a2a_in[rowq:rowq + 1, :], in_=qb_sb[0:1, :]),
                  waits=[("ACT", ia2[1])], incs=[inc("LD", 16)])
                ck_q2r[ft] = CK("sync", "LD", "LDC")
                last_q2r_ck = ck_q2r[ft]
            else:
                s0 = RS_NTMP + 2 * nidx
                iv = inc("VS")
                I("v", lambda s0=s0: nc.vector.reduce_max(
                    rowscr[0:1, s0:s0 + 1], pC[0:1, 0:512], axis=AX.X),
                  waits=[("MM", mm_norm[ft])], incs=[iv])
                iv2 = inc("VS")
                I("v", lambda s0=s0: nc.vector.reduce_max(
                    rowscr[0:1, s0 + 1:s0 + 2], pC[0:1, 512:1024], axis=AX.X),
                  incs=[iv2])
                vs_norm[ft] = iv2[1]
            norm_fts.append(ft)
    act_qkv_all = cnt["ACT"]
    vs_norm_all = vs_norm[norm_fts[-1]]
    W("s", ("SCA", cnt["SCA"]), ("SCB", cnt["SCB"]))
    isc = inc("SCC")
    I("s", lambda: nc.scalar.nop(), incs=[isc])
    scc_contrib = isc[1]

    ivv = None
    kfts = [ft for ft in norm_fts if ft >= 16]
    for i, ft in enumerate(kfts):
        nidx = norm_fts.index(ft)
        h = ft - 16
        slot = RS_MAXK + h
        s0 = RS_NTMP + 2 * nidx
        ivv = inc("VEC") if i == len(kfts) - 1 else None
        I("v", lambda slot=slot, s0=s0: nc.vector.tensor_max(
            rowscr[0:1, slot:slot + 1], rowscr[0:1, s0:s0 + 1], rowscr[0:1, s0 + 1:s0 + 2]),
          waits=[("VS", vs_norm_all)] if i == 0 else [], incs=[ivv])
    vec_norm_all = ivv[1]

    iv = inc("VS")
    I("v", lambda: nc.vector.tensor_copy(m8_b[0:8, :], m8_f[0:8, :]),
      waits=[("LDC", ck_ld1)], incs=[iv])
    vs_m8b = iv[1]
    iv = inc("VS")
    I("v", lambda: nc.vector.tensor_mul(m8_f[0:8, :], m8_f[0:8, :], m8_f[0:8, :]),
      waits=[("VS", vs_m8b)], incs=[iv])
    vs_msq = iv[1]
    ivv = inc("VEC")
    I("v", lambda: nc.vector.reduce_sum(stats[0:8, 13:14], m8_f[0:8, :], axis=AX.X),
      waits=[("VS", vs_msq)], incs=[ivv])
    vec_mm = ivv[1]
    for n in range(8):
        I("sync", lambda n=n: nc.sync.dma_start(out=rowscr[0:1, RS_MMS + 8 * n:RS_MMS + 8 * n + 1],
                                                in_=stats[n:n + 1, 13:14],
                                                allow_slow_non_contiguous=True),
          waits=[("VEC", vec_mm)] if n == 0 else [], incs=[inc("LD", 16)])
    ck_mmrow = CK("sync", "LD", "LDC")
    iv = inc("VS")
    I("v", lambda: nc.vector.tensor_copy(
        rowscr[0:1, RS_MM:RS_MM + 8],
        rowscr[0:1, RS_MMS:RS_MMS + 64].rearrange("a (j q) -> a j q", q=8)[:, :, 0:1].squeeze(2)),
      waits=[("LDC", ck_mmrow)], incs=[iv])
    vs_mmrow = iv[1]

    vec_mtr = {}
    for half in range(2):
        for j in range(8):
            cc = half * 8 + j
            wts = []
            if cc == 0:
                wts = [("VS", vs_m8b), ("VEC", vec_fm_all)]
            elif j == 0:
                wts = [("VEC", vec_mtr[0])]
            im = inc("MM") if j == 7 else None
            I("t", lambda cc=cc, j=j: nc.tensor.transpose(
                pD_bf[:, j * 8:(j + 1) * 8],
                m8_b[0:8, cc * 128:(cc + 1) * 128], ident_sb[0:8, 0:8]),
              waits=wts, incs=[im])
            if im:
                mm_m = im[1]
        ivv = inc("VEC")

        def evac_m(half=half):
            src = pD_bf[:, 0:64].rearrange("p (j f) -> p j f", j=8)
            dst = xn_fm.rearrange("p (j r) -> p j r", r=1040)[
                :, half * 8:(half + 1) * 8, 1024:1032]
            return nc.vector.tensor_copy(dst, src)
        I("v", evac_m, waits=[("MM", mm_m)], incs=[ivv])
        vec_mtr[half] = ivv[1]

    I("sync", lambda: nc.sync.dma_start(out=m8_f[0:8, :], in_=debias8),
      waits=[("VEC", vec_mm)], incs=[inc("LD", 16)])
    ck_deb = CK("sync", "LD", "LDC")
    iv = inc("VS")
    I("v", lambda: nc.vector.tensor_copy(m8_b[0:8, :], m8_f[0:8, :]),
      waits=[("LDC", ck_deb), ("VEC", vec_mtr[1])], incs=[iv])
    vs_debb = iv[1]
    vec_dtr = {}
    for half in range(2):
        for j in range(8):
            cc = half * 8 + j
            wts = []
            if cc == 0:
                wts = [("VS", vs_debb)]
            elif j == 0:
                wts = [("VEC", vec_dtr[0])]
            im = inc("MM") if j == 7 else None
            I("t", lambda cc=cc, j=j: nc.tensor.transpose(
                pE_bf[:, j * 8:(j + 1) * 8],
                m8_b[0:8, cc * 128:(cc + 1) * 128], ident_sb[0:8, 0:8]),
              waits=wts, incs=[im])
            if im:
                mm_d = im[1]
        ivv = inc("VEC")

        def evac_d(half=half):
            src = pE_bf[:, 0:64].rearrange("p (j f) -> p j f", j=8)
            dst = xn_fm.rearrange("p (j r) -> p j r", r=1040)[
                :, half * 8:(half + 1) * 8, 1032:1040]
            return nc.vector.tensor_copy(dst, src)
        I("v", evac_d, waits=[("MM", mm_d)], incs=[ivv])
        vec_dtr[half] = ivv[1]
    vec_xm_all = cnt["VEC"]


    I("gp", lambda: nc.gpsimd.dma_start(out=dgnb_sb[0:72, :], in_=dgnb),
      waits=[("MM", mm_qkv[NFT - 2])], incs=[inc("GP", 16)])
    ck_dgnb = CK("gp", "GP", "GPC")

    mm_ffnA = [0] * 9
    vec_ffnA = [0] * 9
    for tile in range(9):
        par = tile % 2
        ps = psum_qkv[par]
        ncols = 128 if tile < 8 else 16
        wts0 = [("GPC", ck_ffnw), ("ACT", act_qkv_all)]
        if tile == 0:
            wts0.append(("VEC", vec_xm_all))
        if tile >= 2:
            wts0.append(("VEC", vec_ffnA[tile - 2]))
        for cc in range(NCC):
            colbase = cc * 1040 + tile * 128 if tile < 8 else cc * 1040 + 1024
            im = inc("MM") if cc == NCC - 1 else None
            I("t", lambda ps=ps, cc=cc, ncols=ncols, colbase=colbase: nc.tensor.matmul(
                ps[0:ncols, 0:136],
                xn_fm[:, colbase: colbase + ncols],
                ffn_w_sb[:, cc * 136:(cc + 1) * 136],
                start=(cc == 0), stop=(cc == NCC - 1)),
              waits=wts0 if cc == 0 else [], incs=[im])
        mm_ffnA[tile] = cnt["MM"]
        ivv = inc("VEC")
        if tile < 8:
            I("v", lambda ps=ps, tile=tile: nc.vector.tensor_copy(
                A_sb[:, tile * 136:(tile + 1) * 136], ps[0:128, 0:136]),
              waits=[("MM", mm_ffnA[tile])], incs=[ivv])
        else:
            I("v", lambda ps=ps: nc.vector.tensor_copy(cross[0:16, 0:136], ps[0:16, 0:136]),
              waits=[("MM", mm_ffnA[tile])], incs=[ivv])
        vec_ffnA[tile] = ivv[1]
    vec_cross = vec_ffnA[8]

    for n in range(8):
        I("gp", lambda n=n: nc.gpsimd.dma_start(
            out=rowscr[0:1, RS_CROW + 8 * n: RS_CROW + 8 * n + 8],
            in_=cross[n:n + 1, 8 * n:8 * n + 8], allow_slow_non_contiguous=True),
          waits=[("VEC", vec_cross)] if n == 0 else [], incs=[inc("GP", 16)])
        I("gp", lambda n=n: nc.gpsimd.dma_start(
            out=rowscr[0:1, RS_CUROW + 8 * n: RS_CUROW + 8 * n + 8],
            in_=cross[8 + n:9 + n, 72 + 8 * n:72 + 8 * n + 8], allow_slow_non_contiguous=True),
          incs=[inc("GP", 16)])
    ck_rows = CK("gp", "GP", "GPC")

    ia2 = inc("ACT")
    I("s", lambda: nc.scalar.activation(out=rowscr[0:1, RS_MAXK:RS_MAXK + 16],
                                        in_=rowscr[0:1, RS_MAXK:RS_MAXK + 16],
                                        func=AF.Sqrt),
      waits=[("VEC", vec_norm_all)], incs=[ia2])
    ivv = inc("VEC")
    I("v", lambda: nc.vector.tensor_copy(
        aux_bf[0:1, 0:16], rowscr[0:1, RS_MAXK:RS_MAXK + 16]),
      waits=[("ACT", ia2[1])], incs=[ivv])
    vec_aux = ivv[1]
    for j in range(NCORES):
        row = SHROWS * j + 770
        I("sync", lambda j=j, row=row: nc.sync.dma_start(
            out=a2a_in[row:row + 1, 0:2], in_=aux_bf[0:1, 2 * j:2 * j + 2],
            allow_slow_non_contiguous=True),
          waits=[("VEC", vec_aux)] if j == 0 else [], incs=[inc("LD", 16)])
    ck_aux = CK("sync", "LD", "LDC")

    ic = inc("CC")
    I("gp", lambda: nc.gpsimd.collective_compute(
        "AllGather", ALU.bypass, replica_groups=[list(range(NCORES))],
        ins=[a2a_in[:, :]], outs=[a2a_out[:, :]]),
      waits=[("LDC", ck_aux), ("SCC", scc_contrib)], incs=[ic])
    ic = inc("CC")
    I("gp", lambda: nc.gpsimd.collective_compute(
        "AllGather", ALU.bypass, replica_groups=[list(range(NCORES))],
        ins=[fence_in[:, :]], outs=[fence_out[:, :]]),
      waits=[("CC", ic[1] - 1)], incs=[ic])
    cc_qkv = ic[1]

    bclist = [
        (c_bc[:, :], rowscr[0:1, RS_CROW:RS_CROW + 64], 64, [("GPC", ck_rows)]),
        (cu_bc[:, :], rowscr[0:1, RS_CUROW:RS_CUROW + 64], 64, []),
        (mm_bcv[:, :], rowscr[0:1, RS_MM:RS_MM + 8], 8, [("VS", vs_mmrow)]),
        (cen_bc[:, :], misc_sb[0:1, 0:8], 8, [("LDC", ck_ld1)]),
        (slo_bc[:, :], misc_sb[0:1, 8:16], 8, []),
    ]
    vec_bcs = {}
    for i, (dst, srcrow, n, extra) in enumerate(bclist):
        off = 0 if i % 2 == 0 else 512
        wts = list(extra) + [("GS", gs_ones), ("VEC", vec_norm_all)]
        if i >= 2:
            wts.append(("VEC", vec_bcs[i - 2]))
        im = inc("MM")
        I("t", lambda srcrow=srcrow, n=n, off=off: nc.tensor.matmul(
            pC[:, off:off + n], ones_row_f[:, :], srcrow, start=True, stop=True),
          waits=wts, incs=[im])
        ivv = inc("VEC")
        I("v", lambda dst=dst, n=n, off=off: nc.vector.tensor_copy(dst, pC[:, off:off + n]),
          waits=[("MM", im[1])], incs=[ivv])
        vec_bcs[i] = ivv[1]
    prev_bp_mm = 0
    for g in range(4):
        wts = [] if g == 0 else [("MM", prev_bp_mm)]
        I("sync", lambda g=g: nc.sync.dma_start(out=rowscr[0:1, 0:512],
                                                in_=b_proj_in[0:1, 512 * g:512 * (g + 1)]),
          waits=wts, incs=[inc("LD", 16)])
        ck_bp = CK("sync", "LD", "LDC")
        im = inc("MM")
        wts = [("LDC", ck_bp), ("GS", gs_ones)]
        wts.append(("VEC", vec_bcs[4] if g == 0 else vec_bcs[len(bclist) + g - 1]))
        I("t", lambda: nc.tensor.matmul(pC[:, 0:512], ones_row_f[:, :], rowscr[0:1, 0:512],
                                        start=True, stop=True),
          waits=wts, incs=[im])
        prev_bp_mm = im[1]
        ivv = inc("VEC")
        I("v", lambda g=g: nc.vector.tensor_copy(bproj_bc[:, 512 * g:512 * (g + 1)], pC[:, 0:512]),
          waits=[("MM", im[1])], incs=[ivv])
        vec_bcs[len(bclist) + g] = ivv[1]
    vec_bc_all = cnt["VEC"]

    vec_ffn_done = [0] * 8
    vec_utr_prev = 0
    for tt in range(8):
        At = A_sb[:, tt * 136:(tt + 1) * 136]
        iv = inc("VS")
        I("v", lambda At=At: nc.vector.tensor_sub(ffn_tmp[:, 0:64], At[:, 0:64], c_bc[:, :]),
          waits=[("VEC", vec_bc_all), ("VEC", vec_ffnA[tt])], incs=[iv])
        g0 = iv[1]
        iv = inc("VS")
        I("v", lambda: nc.vector.tensor_mul(ffn_tmp[:, 64:128], ffn_tmp[:, 0:64], ffn_tmp[:, 0:64]),
          waits=[("VS", g0)], incs=[iv])
        g1 = iv[1]
        for n in range(8):
            iv = inc("VS") if n == 7 else None
            I("v", lambda n=n: nc.vector.reduce_sum(
                ffn_tmp[:, 128 + n:129 + n], ffn_tmp[:, 64 + 8 * n:64 + 8 * n + 8], axis=AX.X),
              waits=[("VS", g1)] if n == 0 else [], incs=[iv])
        g2 = cnt["VS"]
        iv = inc("VS")
        I("v", lambda At=At, tt=tt: nc.vector.tensor_scalar(
            ffn_tmp[:, 136:144], At[:, 64:72], -2.0, sq_ffn[:, tt:tt + 1], ALU.mult, ALU.add),
          waits=[("VS", g2)], incs=[iv])
        g3 = iv[1]
        iv = inc("VS")
        I("v", lambda: nc.vector.tensor_add(ffn_tmp[:, 144:152], ffn_tmp[:, 136:144], mm_bcv[:, :]),
          waits=[("VS", g3)], incs=[iv])
        g4 = iv[1]
        iv = inc("VS")
        I("v", lambda: nc.vector.reciprocal(ffn_tmp[:, 152:160], ffn_tmp[:, 144:152]),
          waits=[("VS", g4)], incs=[iv])
        g5 = iv[1]
        iv = inc("VS")
        I("v", lambda: nc.vector.tensor_mul(ffn_tmp[:, 160:168], ffn_tmp[:, 128:136], ffn_tmp[:, 152:160]),
          waits=[("VS", g5)], incs=[iv])
        g6 = iv[1]
        iv = inc("VS")
        I("v", lambda: nc.vector.tensor_sub(ffn_tmp[:, 168:176], ffn_tmp[:, 160:168], cen_bc[:, :]),
          waits=[("VS", g6)], incs=[iv])
        g7 = iv[1]
        iv = inc("VS")
        I("v", lambda: nc.vector.tensor_mul(ffn_tmp[:, 176:184], ffn_tmp[:, 168:176], slo_bc[:, :]),
          waits=[("VS", g7)], incs=[iv])
        g8 = iv[1]
        ia = inc("ACT")
        I("s", lambda: nc.scalar.activation(out=ffn_tmp[:, 184:192], in_=ffn_tmp[:, 176:184], func=AF.Sigmoid),
          waits=[("VS", g8)], incs=[ia])
        act_gate = ia[1]
        iv = inc("VS")
        I("v", lambda tt=tt: nc.vector.reduce_max(gmax_all[:, tt:tt + 1], ffn_tmp[:, 184:192], axis=AX.X),
          waits=[("ACT", act_gate)], incs=[iv])
        g9 = iv[1]
        iv = inc("VS")
        I("v", lambda tt=tt: nc.vector.tensor_scalar(
            ffn_tmp[:, 192:200], ffn_tmp[:, 184:192], gmax_all[:, tt:tt + 1], None, ALU.is_ge),
          waits=[("VS", g9)], incs=[iv])
        g10 = iv[1]
        iv = inc("VS")
        I("v", lambda: nc.vector.tensor_scalar_mul(ffn_tmp[:, 200:201], ffn_tmp[:, 192:193], 0.0),
          waits=[("VS", g10)], incs=[iv])
        gpx = iv[1]
        for n in range(1, 8):
            iv = inc("VS")
            I("v", lambda n=n: nc.vector.tensor_add(
                ffn_tmp[:, 200 + n:201 + n], ffn_tmp[:, 199 + n:200 + n], ffn_tmp[:, 191 + n:192 + n]),
              waits=[("VS", gpx)], incs=[iv])
            gpx = iv[1]
        g11 = cnt["VS"]
        iv = inc("VS")
        I("v", lambda: nc.vector.tensor_scalar(
            ffn_tmp[:, 208:216], ffn_tmp[:, 200:208], 0.0, None, ALU.is_equal),
          waits=[("VS", g11)], incs=[iv])
        g12 = iv[1]
        iv = inc("VS")
        I("v", lambda: nc.vector.tensor_mul(ffn_tmp[:, 216:224], ffn_tmp[:, 192:200], ffn_tmp[:, 208:216]),
          waits=[("VS", g12)], incs=[iv])
        g13 = iv[1]
        iv = inc("VS")
        I("v", lambda At=At: nc.vector.tensor_sub(ffn_tmp[:, 0:64], At[:, 72:136], cu_bc[:, :]),
          waits=[("VS", g13)], incs=[iv])
        g14 = iv[1]
        for n in range(8):
            iv = inc("VS") if n == 7 else None
            I("v", lambda n=n: nc.vector.tensor_scalar(
                umt[:, 8 * n:8 * n + 8], ffn_tmp[:, 8 * n:8 * n + 8],
                ffn_tmp[:, 216 + n:217 + n], None, ALU.mult),
              waits=[("VS", g14)] if n == 0 else [], incs=[iv])
        g15 = cnt["VS"]
        ivv = inc("VEC")
        I("v", lambda: nc.vector.tensor_copy(umt[:, 64:72], ffn_tmp[:, 216:224]),
          waits=[("VS", g15)], incs=[ivv])
        vec_umt = ivv[1]
        im = inc("MM")
        wts = [("VEC", vec_umt)]
        wts.append(("VEC", vec_utr_prev if tt >= 1 else vec_mtr[1]))
        I("t", lambda: nc.tensor.transpose(pD_bf[0:72, 0:128], umt[:, 0:72], ident_sb[:, :]),
          waits=wts, incs=[im])
        ivv = inc("VEC")
        I("v", lambda tt=tt: nc.vector.tensor_copy(u_fm[0:72, tt * 128:(tt + 1) * 128], pD_bf[0:72, 0:128]),
          waits=[("MM", im[1])], incs=[ivv])
        vec_utr_prev = ivv[1]
        mm_mod = {}
        for fg in range(4):
            ps = [pA[:, 0:512], pA[:, 512:1024], pB[:, 0:512], pB[:, 512:1024]][fg]
            wts = [("VEC", vec_utr_prev), ("GPC", ck_dgnb)]
            wts.append(("VEC", vec_ffn_done[tt - 1] if tt >= 1 else vec_ffnA[8]))
            im = inc("MM")
            I("t", lambda ps=ps, tt=tt, fg=fg: nc.tensor.matmul(
                ps, u_fm[0:72, tt * 128:(tt + 1) * 128],
                dgnb_sb[0:72, 512 * fg:512 * (fg + 1)], start=True, stop=True),
              waits=wts, incs=[im])
            mm_mod[fg] = im[1]
        for fg in range(4):
            ps = [pA[:, 0:512], pA[:, 512:1024], pB[:, 0:512], pB[:, 512:1024]][fg]
            xnsl = xn_tm[:, tt * D + 512 * fg: tt * D + 512 * (fg + 1)]
            asl = acc[:, tt * D + 512 * fg: tt * D + 512 * (fg + 1)]
            iv = inc("VS")
            I("v", lambda ps=ps, xnsl=xnsl: nc.vector.tensor_sub(ffn_tmp[:, 0:512], ps, xnsl),
              waits=[("MM", mm_mod[fg])], incs=[iv])
            ivv = inc("VEC")
            I("v", lambda asl=asl, tt=tt: nc.vector.scalar_tensor_tensor(
                asl, ffn_tmp[:, 0:512], gmax_all[:, tt:tt + 1], asl, ALU.mult, ALU.add),
              waits=[("VS", iv[1])], incs=[ivv])
        ivv = inc("VEC")
        I("v", lambda tt=tt: nc.vector.tensor_add(
            acc[:, tt * D:(tt + 1) * D], acc[:, tt * D:(tt + 1) * D], bproj_bc[:, :]),
          incs=[ivv])
        vec_ffn_done[tt] = ivv[1]
    vec_ffn_all = cnt["VEC"]
    vec_utr_last = vec_utr_prev

    def pid_init():
        pid_holder["sync"] = nc.partition_id(engines=[mybir.EngineType.SP])
        return nc.sync.nop()
    I("sync", pid_init)

    def qkv_slab():
        pid = pid_holder["sync"]
        v = a2a_out.rearrange("(i s r) t -> i s r t", s=NCORES, r=SHROWS)
        src_ = v[0:NCORES, bass.ts(pid, 1), :, :].squeeze(1)
        dst_ = my_qkv.rearrange("(i r) t -> i r t", r=SHROWS)
        return nc.sync.dma_start(out=dst_, in_=src_)
    I("sync", qkv_slab, waits=[("CC", cc_qkv)], incs=[inc("LD", 16)])
    ck_slab = CK("sync", "LD", "LDC")

    a2v = my_qkv.rearrange("(i r) t -> i r t", r=SHROWS)

    def a2v_my(i0, i1, r0, r1):
        return a2v[i0:i1, r0:r1, :].unsqueeze(1)

    for j in range(NCORES):
        I("sync", lambda j=j: nc.sync.dma_start(
            out=rowscr[0:1, RS_AUXB + 8 * j:RS_AUXB + 8 * j + 1].bitcast(BF16),
            in_=a2v_my(j, j + 1, 770, 771)[:, :, :, 0:2].squeeze(2).squeeze(1),
            allow_slow_non_contiguous=True),
          incs=[inc("LD", 16)])
    ck_auxout = CK("sync", "LD", "LDC")
    iv = inc("VS")
    I("v", lambda: nc.vector.tensor_copy(
        rowscr[0:1, RS_AUXF:RS_AUXF + 16].rearrange("a (j h) -> a j h", h=2),
        rowscr[0:1, RS_AUXB:RS_AUXB + 64].bitcast(BF16).rearrange(
            "a (j q) -> a j q", q=16)[:, :, 0:2]),
      waits=[("LDC", ck_auxout)], incs=[iv])
    g0 = iv[1]
    AF_ = RS_AUXF
    PM = RS_T0
    iv = inc("VS")
    I("v", lambda: nc.vector.tensor_copy(rowscr[0:1, PM:PM + 2], rowscr[0:1, AF_:AF_ + 2]),
      waits=[("VS", g0)], incs=[iv])
    gp_ = iv[1]
    for j in range(1, NCORES):
        iv = inc("VS")
        I("v", lambda j=j: nc.vector.tensor_max(
            rowscr[0:1, PM + 2 * j:PM + 2 * j + 2],
            rowscr[0:1, PM + 2 * (j - 1):PM + 2 * j],
            rowscr[0:1, AF_ + 2 * j:AF_ + 2 * j + 2]),
          waits=[("VS", gp_)], incs=[iv])
        gp_ = iv[1]
    ivv = inc("VEC")
    I("v", lambda: nc.vector.tensor_scalar_mul(rowscr[0:1, RS_NM:RS_NM + 16],
                                               rowscr[0:1, PM:PM + 16], -1.0),
      waits=[("VS", gp_)], incs=[ivv])
    vec_negm = ivv[1]

    vstage = [p_sb[:, 0:1024], bc_sb.bitcast(BF16)[:, 0:1024]]
    ck_vst = {}
    mm_vtr = {}
    vec_vtr = {}
    for hl in range(2):
        for j in range(NCORES):
            k = hl * 8 + j
            par = k % 2
            wts = []
            if k >= 2:
                wts.append(("MM", mm_vtr[k - 2]))
            I("sync", lambda hl=hl, j=j, par=par: nc.sync.dma_start(
                out=vstage[par][:, :],
                in_=a2v_my(j, j + 1, 512 + hl * 128, 512 + (hl + 1) * 128).squeeze(1).squeeze(0)),
              waits=wts, incs=[inc("LD", 16)])
            ck_vst[k] = CK("sync", "LD", "LDC")
            pbuf = pD_bf if par == 0 else pE_bf
            for ttv in range(8):
                wts = []
                if ttv == 0:
                    wts = [("LDC", ck_vst[k])]
                    if k == 0:
                        wts.append(("VEC", vec_utr_last))
                    elif k == 1:
                        wts.append(("VEC", vec_xm_all))
                    else:
                        wts.append(("VEC", vec_vtr[k - 2]))
                im = inc("MM") if ttv == 7 else None
                I("t", lambda par=par, ttv=ttv, pbuf=pbuf: nc.tensor.transpose(
                    pbuf[:, ttv * 128:(ttv + 1) * 128], vstage[par][:, ttv * 128:(ttv + 1) * 128],
                    ident_sb[:, :]),
                  waits=wts, incs=[im])
            mm_vtr[k] = cnt["MM"]
            ivv = inc("VEC")
            I("v", lambda hl=hl, j=j, pbuf=pbuf: nc.vector.tensor_copy(
                V_tm[:, hl * 8192 + j * 1024: hl * 8192 + (j + 1) * 1024], pbuf[:, 0:1024]),
              waits=[("MM", mm_vtr[k])], incs=[ivv])
            vec_vtr[k] = ivv[1]
    vec_vtr_all = cnt["VEC"]

    for hl in range(2):
        wts = [("VEC", vec_ffn_all), ("SCC", scc_contrib)] if hl == 0 else []
        I("sync", lambda hl=hl: nc.sync.dma_start(
            out=K_sb.rearrange("p (h j t) -> p h j t", h=2, j=8)[:, hl:hl + 1, :, :],
            in_=a2v_my(0, NCORES, 256 + hl * 128, 256 + (hl + 1) * 128).squeeze(1)
                .transpose([1, 0, 2]).unsqueeze(1)),
          waits=wts, incs=[inc("LD", 16)])
        I("sync", lambda hl=hl: nc.sync.dma_start(
            out=Q_sb.rearrange("p (h j t) -> p h j t", h=2, j=8)[:, hl:hl + 1, :, :],
            in_=a2v_my(0, NCORES, hl * 128, (hl + 1) * 128).squeeze(1)
                .transpose([1, 0, 2]).unsqueeze(1)),
          incs=[inc("LD", 16)])
    ck_kq = CK("sync", "LD", "LDC")

    def q2p_dma(m, waits=()):
        hl_, b_ = m // NQB, m % NQB
        rowq = SHROWS * (b_ // 2) + 768 + hl_
        c0 = (b_ % 2) * 512
        ii = inc(SCP[m % 2], 16)
        I("s", lambda rowq=rowq, c0=c0: nc.scalar.dma_start(
            out=qb_sb[0:1, 0:512], in_=my_qkv[rowq:rowq + 1, c0:c0 + 512]),
          waits=list(waits), incs=[ii])
        return ii[1]

    sp = [pA[:, 0:512], pA[:, 512:1024], pD[:, 0:512]]
    obank = [pB[:, 0:512], pC[:, 0:512]]
    sbank = [pB[0:1, 512:1024], pC[0:1, 512:1024]]
    bbc_sb = wp_buf.bitcast(F32)
    mm_kq_of = {}
    vec_badd_of = {}
    act_exp_of = {}
    mm_cons_of = {}
    vec_yev = {}
    vec_bcev = 0
    vec_bbc = 0
    pd_guard = 0
    chunk_ctr = 0
    prev_blk_mm = 0
    sc_q2p = {0: q2p_dma(0, waits=[("LDC", ck_slab)])}
    for hl in range(2):
        for b in range(NQB):
            m = hl * NQB + b
            pbk = b % 2
            nchunks = 4 * b + 4
            isc = inc("SCC")
            I("s", lambda: nc.scalar.nop(), waits=[(SCP[m % 2], sc_q2p[m])], incs=[isc])
            scc_q2p = isc[1]
            iv = inc("VS")
            nmcol = RS_NM + 2 * (b // 2) + hl
            wts = [("SCC", scc_q2p)]
            if prev_blk_mm:
                wts.append(("MM", prev_blk_mm))
            if m == 0:
                wts.append(("VEC", vec_negm))
            I("v", lambda nmcol=nmcol: nc.vector.tensor_scalar(
                qb_sb[0:1, 512:1024], qb_sb[0:1, 0:512],
                rowscr[0:1, nmcol:nmcol + 1], SHIFT, ALU.mult, ALU.add),
              waits=wts, incs=[iv])
            vs_B = iv[1]
            if m + 1 < 2 * NQB:
                sc_q2p[m + 1] = q2p_dma(m + 1, waits=[("VS", vs_B)])
            im = inc("MM")
            wts = [("VS", vs_B)]
            if m == 0:
                wts += [("LDC", ck_kq), ("VEC", vec_vtr_all)]
            else:
                wts.append(("VEC", vec_bbc))
            I("t", lambda: nc.tensor.matmul(pE[:, 0:512], ones_row_bf[:, :],
                                            qb_sb[0:1, 512:1024], start=True, stop=True),
              waits=wts, incs=[im])
            mm_bbc = im[1]
            ivv = inc("VEC")
            I("v", lambda: nc.vector.tensor_copy(bbc_sb[:, :], pE[:, 0:512]),
              waits=[("MM", mm_bbc)], incs=[ivv])
            vec_bbc = ivv[1]
            def emit_pv(gi, kcl, hl=hl, b=b, pbk=pbk, m=m, nchunks=nchunks):
                j0 = max(0, 128 * kcl - 512 * b)
                first = (kcl == 0)
                last = (kcl == nchunks - 1)
                p2 = gi % 2
                wts = [("ACT", act_exp_of[gi])]
                if first and m >= 2:
                    wts.append(("VEC", vec_yev[m - 2]))
                I("t", lambda: nc.tensor.matmul(
                    obank[pbk][:, j0:512],
                    V_tm[:, hl * 8192 + kcl * 128: hl * 8192 + (kcl + 1) * 128],
                    p_sb[:, p2 * 512 + j0: p2 * 512 + 512],
                    start=first, stop=last), waits=wts)
                im = inc("MM")
                I("t", lambda: nc.tensor.matmul(
                    sbank[pbk][:, j0:512], ones_col_bf[:, :],
                    p_sb[:, p2 * 512 + j0: p2 * 512 + 512],
                    start=first, stop=last), incs=[im])
                mm_cons_of[gi] = im[1]

            g0blk = chunk_ctr
            for kc in range(nchunks):
                gi = chunk_ctr
                chunk_ctr += 1
                s3 = gi % 3
                p2 = gi % 2
                j0 = max(0, 128 * kc - 512 * b)
                diag = (kc >= 4 * b)
                wts = []
                if gi >= 3:
                    wts.append(("ACT", act_exp_of[gi - 3]))
                else:
                    wts.append(("VEC", vec_vtr_all))
                if s3 == 2 and pd_guard:
                    wts.append(("VEC", pd_guard))
                    pd_guard = 0
                im = inc("MM")
                I("t", lambda hl=hl, b=b, kc=kc, s3=s3, j0=j0: nc.tensor.matmul(
                    sp[s3][:, j0:512],
                    K_sb[:, hl * 8192 + kc * 128: hl * 8192 + (kc + 1) * 128],
                    Q_sb[:, hl * 8192 + 512 * b + j0: hl * 8192 + 512 * (b + 1)],
                    start=True, stop=True), waits=wts, incs=[im])
                mm_kq_of[gi] = im[1]
                ivv = inc("VEC")
                wts = [("MM", mm_kq_of[gi])]
                if kc == 0:
                    wts.append(("VEC", vec_bbc))
                I("v", lambda s3=s3, j0=j0: nc.vector.tensor_add(
                    sp[s3][:, j0:512], sp[s3][:, j0:512], bbc_sb[:, j0:512]),
                  waits=wts, incs=[ivv])
                if diag:
                    ivv = inc("VEC")
                    I("v", lambda s3=s3, j0=j0: nc.vector.tensor_add(
                        sp[s3][:, j0:j0 + 128], sp[s3][:, j0:j0 + 128], causal_sb[:, :]),
                      incs=[ivv])
                vec_badd_of[gi] = ivv[1]
                if kc >= 2:
                    emit_pv(g0blk + kc - 2, kc - 2)
                wts = [("VEC", vec_badd_of[gi])]
                if gi >= 2:
                    wts.append(("MM", mm_cons_of[gi - 2]))
                ia = inc("ACT")
                I("s", lambda s3=s3, p2=p2, j0=j0: nc.scalar.activation(
                    out=p_sb[:, p2 * 512 + j0: p2 * 512 + 512], in_=sp[s3][:, j0:512],
                    func=AF.Exp),
                  waits=wts, incs=[ia])
                act_exp_of[gi] = ia[1]
            emit_pv(g0blk + nchunks - 2, nchunks - 2)
            emit_pv(g0blk + nchunks - 1, nchunks - 1)
            mm_last = mm_cons_of[chunk_ctr - 1]
            prev_blk_mm = mm_last
            iv = inc("VS")
            I("v", lambda pbk=pbk: nc.vector.reciprocal(rowscr[0:1, 0:512], sbank[pbk]),
              waits=[("MM", mm_last)], incs=[iv])
            g_r = iv[1]
            im = inc("MM")
            wts = [("VS", g_r), ("ACT", act_exp_of[chunk_ctr - 1])]
            if vec_bcev:
                wts.append(("VEC", vec_bcev))
            I("t", lambda: nc.tensor.matmul(pD[:, 0:512], ones_row_f[:, :], rowscr[0:1, 0:512],
                                            start=True, stop=True), waits=wts, incs=[im])
            ivv = inc("VEC")
            I("v", lambda: nc.vector.tensor_copy(bc_sb[:, :], pD[:, 0:512]),
              waits=[("MM", im[1])], incs=[ivv])
            vec_bcev = ivv[1]
            pd_guard = vec_bcev
            ivv = inc("VEC")
            I("v", lambda hl=hl, b=b, pbk=pbk: nc.vector.tensor_mul(
                y_sb[:, hl * 8192 + 512 * b: hl * 8192 + 512 * (b + 1)], obank[pbk], bc_sb[:, :]),
              waits=[("VEC", vec_bcev)], incs=[ivv])
            vec_yev[m] = ivv[1]
    vec_y_all = cnt["VEC"]
    act_exp_all = act_exp_of[chunk_ctr - 1]

    for hl in range(2):
        I("sync", lambda hl=hl: nc.sync.dma_start(
            out=y_in.rearrange("(j l p) t -> p l j t", l=2, p=128)[:, hl:hl + 1, :, :],
            in_=y_sb.rearrange("p (h j t) -> p h j t", h=2, j=8)[:, hl:hl + 1, :, :]),
          waits=[("VEC", vec_y_all)] if hl == 0 else [], incs=[inc("LD", 16)])
    ck_y = CK("sync", "LD", "LDC")
    ic = inc("CC")
    I("gp", lambda: nc.gpsimd.collective_compute(
        "AllGather", ALU.bypass, replica_groups=[list(range(NCORES))],
        ins=[y_in[:, :]], outs=[y_out[:, :]]),
      waits=[("LDC", ck_y)], incs=[ic])
    ic = inc("CC")
    I("gp", lambda: nc.gpsimd.collective_compute(
        "AllGather", ALU.bypass, replica_groups=[list(range(NCORES))],
        ins=[fence_in[:, :]], outs=[fence_out[:, :]]),
      waits=[("CC", ic[1] - 1)], incs=[ic])
    cc_y = ic[1]

    ck_yown = [0] * 16

    def yown_src(fi):
        pid = pid_holder["sync"]
        v = y_out.rearrange("(i s l p) t -> i s l p t", s=NCORES, l=2, p=128)
        return v[fi // 2:fi // 2 + 1, bass.ts(pid, 1), fi % 2:fi % 2 + 1, :, :] \
            .squeeze(2).squeeze(1).squeeze(0)

    for fi in range(16):
        I("sync", lambda fi=fi: nc.sync.dma_start(
            out=yown[:, fi * 1024:(fi + 1) * 1024], in_=yown_src(fi)),
          waits=[("CC", cc_y)] if fi == 0 else [], incs=[inc("LD", 16)])
        ck_yown[fi] = CK("sync", "LD", "LDC")

    qbanks = [pA[:, 0:512], pA[:, 512:1024], pB[:, 0:512], pB[:, 512:1024],
              pC[:, 0:512], pC[:, 512:1024], pD[:, 0:512], pE[:, 0:512]]
    mm_wp_cons = {}
    vec_oev = {}
    ck_out = {}
    ck_wp_of = {}
    GPP = ("GPA", "GPB")
    gpc0 = cnt["GPC"]
    for g in range(4):
        for fi in range(16):
            k = g * 16 + fi
            par = k % 2
            wts = []
            if k >= 2:
                wts.append(("MM", mm_wp_cons[k - 2]))
            I("gp", lambda g=g, fi=fi, par=par: nc.gpsimd.dma_start(
                out=wp_buf[:, par * 512:(par + 1) * 512],
                in_=w_projT[fi * 128:(fi + 1) * 128, g * 512:(g + 1) * 512]),
              waits=wts, incs=[inc(GPP[par], 16)])
            if k >= 1:
                ck_wp_of[k - 1] = CK("gp", GPP[(k - 1) % 2], "GPC")
            if k == 63:
                ck_wp_of[63] = CK("gp", GPP[1], "GPC")
            for qt in range(8):
                wts = []
                if qt == 0:
                    wts += [("GPC", gpc0 + k + 1), ("LDC", ck_yown[fi])]
                if fi == 0:
                    if g == 0:
                        if qt == 0:
                            wts += [("VEC", vec_y_all), ("ACT", act_exp_all), ("VEC", vec_bcev)]
                    else:
                        wts.append(("VEC", vec_oev[(g - 1, qt)]))
                im = inc("MM") if qt == 7 else None
                I("t", lambda fi=fi, qt=qt, par=par: nc.tensor.matmul(
                    qbanks[qt], yown[:, fi * 1024 + qt * 128: fi * 1024 + (qt + 1) * 128],
                    wp_buf[:, par * 512:(par + 1) * 512],
                    start=(fi == 0), stop=(fi == 15)),
                  waits=wts, incs=[im])
            mm_wp_cons[k] = cnt["MM"]
        for qt in range(8):
            k = g * 8 + qt
            par = k % 2
            wts = [("MM", mm_wp_cons[g * 16 + 15])]
            if k >= 2:
                wts.append(("LDC", ck_out[k - 2]))
            ivv = inc("VEC")
            I("v", lambda g=g, qt=qt, par=par: nc.vector.tensor_add(
                out_stage[par][:, 0:512], qbanks[qt],
                acc[:, qt * D + 512 * g: qt * D + 512 * (g + 1)]),
              waits=wts, incs=[ivv])
            vec_oev[(g, qt)] = ivv[1]
            I("sync", lambda g=g, qt=qt, par=par: nc.sync.dma_start(
                out=out_own[qt * 128:(qt + 1) * 128, g * 512:(g + 1) * 512],
                in_=out_stage[par][:, 0:512]),
              waits=[("VEC", vec_oev[(g, qt)])], incs=[inc("LD", 16)])
            ck_out[k] = CK("sync", "LD", "LDC")
    W("sync", ("LD", cnt["LD"]))

    with nc.Block() as block:
        @block.sync
        def _(eng):
            engmap["sync"] = nc.sync
            for fn in prog["sync"]:
                fn()

        @block.gpsimd
        def _(eng):
            engmap["gp"] = nc.gpsimd
            for fn in prog["gp"]:
                fn()

        @block.tensor
        def _(eng):
            engmap["t"] = nc.tensor
            for fn in prog["t"]:
                fn()

        @block.vector
        def _(eng):
            engmap["v"] = nc.vector
            for fn in prog["v"]:
                fn()

        @block.scalar
        def _(eng):
            engmap["s"] = nc.scalar
            for fn in prog["s"]:
                fn()

    return nc


_NC_CACHE = [None]


def kernel(L0_attn_x, L0_attn_w_attn, L0_attn_b_attn, L0_attn_w_proj, L0_attn_b_proj,
           L0_ffn_select_weight, L0_ffn_select_mean_diff, L0_ffn_imp_center,
           L0_ffn_imp_slope, L0_ffn_lora_update, L0_ffn_lora_degen,
           L0_ffn_bias_w, L0_ffn_debias_w):
    bf = ml_dtypes.bfloat16
    x2d = np.ascontiguousarray(np.asarray(L0_attn_x, dtype=np.float32).reshape(T, D))
    w_attnT = np.ascontiguousarray(np.asarray(L0_attn_w_attn, dtype=np.float32).T)
    b_att_st = np.ascontiguousarray(np.asarray(L0_attn_b_attn, dtype=np.float32).reshape(NFT, 128).T)
    w_projT = np.ascontiguousarray(np.asarray(L0_attn_w_proj, dtype=np.float32).T)
    b_proj_in = np.ascontiguousarray(np.asarray(L0_attn_b_proj, dtype=np.float32).reshape(1, D))
    sw = np.asarray(L0_ffn_select_weight, dtype=np.float32)
    md = np.asarray(L0_ffn_select_mean_diff, dtype=np.float32)
    lu = np.asarray(L0_ffn_lora_update, dtype=np.float32)
    ldg = np.asarray(L0_ffn_lora_degen, dtype=np.float32)
    bw = np.asarray(L0_ffn_bias_w, dtype=np.float32)
    dw = np.asarray(L0_ffn_debias_w, dtype=np.float32)
    ffn_w = np.ascontiguousarray(np.concatenate(
        [sw.transpose(1, 0, 2).reshape(D, 64), md.T, lu.transpose(1, 0, 2).reshape(D, 64)], axis=1))
    dgnb = np.ascontiguousarray(np.concatenate([ldg.transpose(0, 2, 1).reshape(64, D), bw], axis=0))
    misc = np.ascontiguousarray(np.concatenate(
        [np.asarray(L0_ffn_imp_center, dtype=np.float32),
         np.asarray(L0_ffn_imp_slope, dtype=np.float32)]).reshape(1, 16))
    ident = np.eye(128, dtype=bf)
    kk, jj = np.meshgrid(np.arange(128), np.arange(128), indexing="ij")
    causal = np.where(jj >= kk, 0.0, NEGBIG).astype(np.float32)

    if _NC_CACHE[0] is None:
        _NC_CACHE[0] = build_program()
    nc = _NC_CACHE[0]

    in_maps = []
    for c in range(NCORES):
        in_maps.append({
            "x_own": np.ascontiguousarray(x2d[c * TPC:(c + 1) * TPC]),
            "w_attnT": w_attnT,
            "b_att_st": b_att_st,
            "w_projT": w_projT,
            "b_proj_in": b_proj_in,
            "ffn_w": ffn_w,
            "dgnb": dgnb,
            "mean8": md,
            "debias8": dw,
            "misc_in": misc,
            "ident_in": ident,
            "causal_in": causal,
        })
    res = run_bass_kernel_spmd(nc, in_maps, core_ids=list(range(NCORES)))
    out = np.concatenate([res.results[c]["out_own"] for c in range(NCORES)], axis=0)
    return out.reshape(1, T, D)


# version 18
# speedup vs baseline: 1.0322x; 1.0184x over previous
import numpy as np
import ml_dtypes
import concourse.bass as bass
import concourse.mybir as mybir
from concourse.bass_utils import run_bass_kernel_spmd

F32 = mybir.dt.float32
BF16 = mybir.dt.bfloat16
AF = mybir.ActivationFunctionType
ALU = mybir.AluOpType
AX = mybir.AxisListType

D = 2048
T = 8192
NCORES = 8
TPC = 1024
NFT = 48
NCC = 16
NQB = 16
INV_SQRT_HD = float(1.0 / np.sqrt(128.0))
EPS = 1e-5
NEGBIG = -1.0e30
SHROWS = 771
SHIFT = 60.0
A2A_ROWS = SHROWS * NCORES


def build_program():
    nc = bass.Bass("TRN2", target_bir_lowering=False)

    x_own = nc.dram_tensor("x_own", [TPC, D], F32, kind="ExternalInput").ap()
    w_attnT = nc.dram_tensor("w_attnT", [D, 3 * D], F32, kind="ExternalInput").ap()
    b_att_st = nc.dram_tensor("b_att_st", [128, NFT], F32, kind="ExternalInput").ap()
    w_projT = nc.dram_tensor("w_projT", [D, D], F32, kind="ExternalInput").ap()
    b_proj_in = nc.dram_tensor("b_proj_in", [1, D], F32, kind="ExternalInput").ap()
    ffn_w = nc.dram_tensor("ffn_w", [D, 136], F32, kind="ExternalInput").ap()
    dgnb = nc.dram_tensor("dgnb", [72, D], F32, kind="ExternalInput").ap()
    mean8 = nc.dram_tensor("mean8", [8, D], F32, kind="ExternalInput").ap()
    debias8 = nc.dram_tensor("debias8", [8, D], F32, kind="ExternalInput").ap()
    misc_in = nc.dram_tensor("misc_in", [1, 16], F32, kind="ExternalInput").ap()
    ident_in = nc.dram_tensor("ident_in", [128, 128], BF16, kind="ExternalInput").ap()
    causal_in = nc.dram_tensor("causal_in", [128, 128], F32, kind="ExternalInput").ap()
    out_own = nc.dram_tensor("out_own", [TPC, D], F32, kind="ExternalOutput").ap()

    a2a_in = nc.dram_tensor("a2a_in", [A2A_ROWS, TPC], BF16).ap()
    a2a_out = nc.dram_tensor("a2a_out", [NCORES * A2A_ROWS, TPC], BF16, addr_space="Shared").ap()
    y_in = nc.dram_tensor("y_in", [D, TPC], BF16).ap()
    y_out = nc.dram_tensor("y_out", [NCORES * D, TPC], BF16, addr_space="Shared").ap()
    my_qkv = nc.dram_tensor("my_qkv", [NCORES * SHROWS, TPC], BF16).ap()
    fence_in = nc.dram_tensor("fence_in", [1, 16], BF16).ap()
    fence_out = nc.dram_tensor("fence_out", [NCORES, 16], BF16, addr_space="Shared").ap()
    pid_holder = {}

    def SB(name, shape, dt):
        return nc.alloc_sbuf_tensor(name, shape, dt).ap()

    acc = SB("acc", [128, 8 * D], F32)
    reg_b = SB("reg_b", [128, 16384], BF16)
    reg_c = SB("reg_c", [128, 16640], BF16)
    reg_d = SB("reg_d", [128, 16384], BF16)
    reg_e = SB("reg_e", [128, 16384], BF16)

    ident_sb = SB("ident_sb", [128, 128], BF16)
    causal_sb = SB("causal_sb", [128, 128], F32)
    b_att_sb = SB("b_att_sb", [128, NFT], F32)
    ones_col_bf = SB("ones_col_bf", [128, 1], BF16)
    ones_col_f = SB("ones_col_f", [128, 1], F32)
    ones_row_f = SB("ones_row_f", [1, 128], F32)
    ones_row_bf = SB("ones_row_bf", [1, 128], BF16)
    qb_sb = SB("qb_sb", [1, 1024], BF16)
    misc_sb = SB("misc_sb", [1, 16], F32)
    rowscr = SB("rowscr", [1, 1024], F32)
    stats = SB("stats", [128, 16], F32)
    sq_ffn = SB("sq_ffn", [128, 8], F32)
    gmax_all = SB("gmax_all", [128, 8], F32)
    p_sb = SB("p_sb", [128, 1024], BF16)
    bc_sb = SB("bc_sb", [128, 512], F32)
    wp_buf = SB("wp_buf", [128, 1024], BF16)
    aux_bf = SB("aux_bf", [1, 64], BF16)

    xn_tm = reg_b
    K_sb = reg_b
    xn_fm = reg_c
    V_tm = reg_c
    Q_sb = reg_d
    y_sb = reg_e
    yown = reg_e

    xsq = reg_d[:, 0:4096].bitcast(F32)
    ktmp = reg_d[:, 0:2048].bitcast(F32)
    wt_buf = [reg_d[:, 4096:6144], reg_d[:, 6144:8192]]
    kqv_stage = [reg_d[:, 8192:9216], reg_d[:, 9216:10240]]
    out_stage = [reg_d[:, 10240:11264].bitcast(F32), reg_d[:, 11264:12288].bitcast(F32)]
    ffn_w_sb = reg_d[:, 12288:14464]
    dgnb_sb = reg_d[:, 4096:6144]

    A_sb = reg_e[:, 0:2448].bitcast(F32)
    umt = reg_e[:, 2448:2520]
    u_fm = reg_e[:, 2560:3584]
    ffn_tmp = reg_e[:, 3584:4608].bitcast(F32)
    bproj_bc = reg_e[:, 4608:8704].bitcast(F32)
    m8_f = reg_e[:, 8704:12800].bitcast(F32)
    m8_b = reg_e[:, 12800:14848]
    cross = reg_e[:, 14848:15392].bitcast(F32)
    c_bc = reg_e[:, 15392:15520].bitcast(F32)
    cu_bc = reg_e[:, 15520:15648].bitcast(F32)
    mm_bcv = reg_e[:, 15648:15664].bitcast(F32)
    cen_bc = reg_e[:, 15664:15680].bitcast(F32)
    slo_bc = reg_e[:, 15680:15696].bitcast(F32)

    RS_RINV = 0
    RS_MAXK = 512
    RS_AUXB = 544
    RS_AUXF = 608
    RS_NM = 624
    RS_T0 = 648
    RS_CROW = 704
    RS_CUROW = 768
    RS_MMS = 832
    RS_MM = 896
    RS_NTMP = 904

    def PSUM(name, shape):
        return nc.alloc_psum_tensor(name, shape, F32).ap()

    pA = PSUM("pA", [128, 1024])
    pB = PSUM("pB", [128, 1024])
    pC = PSUM("pC", [128, 1024])
    pD = PSUM("pD", [128, 512])
    pE = PSUM("pE", [128, 512])
    pD_bf = pD.bitcast(BF16)
    pE_bf = pE.bitcast(BF16)

    prog = {k: [] for k in ("sync", "gp", "t", "v", "s")}
    cnt = {}
    sems = {}
    for s in ("LD", "GP", "MM", "ACT", "VEC", "CC", "VS", "GS",
              "SCA", "SCB", "SCC", "LDC", "GPC", "GPA", "GPB", "SYA", "SYB"):
        sems[s] = nc.alloc_semaphore(s)
        cnt[s] = 0
    engmap = {}

    def inc(sem, n=1):
        cnt[sem] += n
        return (sem, cnt[sem], n)

    def I(eng, fn, waits=(), incs=()):
        waits = tuple(waits)
        incs = tuple(i for i in incs if i is not None)

        def run():
            e = engmap[eng]
            for s, v in waits:
                if v > 0:
                    e.wait_ge(sems[s], v)
            ins = fn()
            for s, _v, n in incs:
                ins = ins.then_inc(sems[s], n)
        prog[eng].append(run)

    def W(eng, *waits):
        waits = tuple(waits)

        def run():
            e = engmap[eng]
            for s, v in waits:
                if v > 0:
                    e.wait_ge(sems[s], v)
        prog[eng].append(run)

    def CK(eng, qsem, cksem):
        v = cnt[qsem]
        ic_ = inc(cksem)

        def run():
            e = engmap[eng]
            e.wait_ge(sems[qsem], v)
            e.nop().then_inc(sems[cksem], 1)
        prog[eng].append(run)
        return ic_[1]

    x_src = x_own.rearrange("(tt p) f -> p tt f", p=128)
    x_dst = acc.rearrange("p (tt f) -> p tt f", f=D)
    I("sync", lambda: nc.sync.dma_start(out=x_dst[:, 0:4, :], in_=x_src[:, 0:4, :]),
      incs=[inc("LD", 16)])
    for dst, src in ((ident_sb[:, :], ident_in), (causal_sb[:, :], causal_in),
                     (b_att_sb[:, :], b_att_st), (misc_sb[:, :], misc_in),
                     (m8_f[0:8, :], mean8)):
        I("sync", lambda dst=dst, src=src: nc.sync.dma_start(out=dst, in_=src),
          incs=[inc("LD", 16)])
    ck_ld1 = CK("sync", "LD", "LDC")
    I("sync", lambda: nc.sync.dma_start(out=x_dst[:, 4:8, :], in_=x_src[:, 4:8, :]),
      incs=[inc("LD", 16)])
    ck_xb = CK("sync", "LD", "LDC")

    for ap_ in (ones_col_bf[:, :], ones_col_f[:, :], ones_row_f[:, :], ones_row_bf[:, :]):
        I("gp", lambda ap_=ap_: nc.gpsimd.memset(ap_, 1.0))
    ig = inc("GS", 1)
    I("gp", lambda: nc.gpsimd.engine_nop(), incs=[ig])
    gs_ones = ig[1]

    ffn_w_src = ffn_w.rearrange("(cc p) f -> p cc f", p=128)
    ffn_w_dst = ffn_w_sb.rearrange("p (cc f) -> p cc f", f=136)
    I("gp", lambda: nc.gpsimd.dma_start(out=ffn_w_dst, in_=ffn_w_src), incs=[inc("GP", 16)])

    iv = inc("VS")
    I("v", lambda: nc.vector.tensor_scalar_mul(b_att_sb[:, 0:16], b_att_sb[:, 0:16], INV_SQRT_HD),
      waits=[("LDC", ck_ld1)], incs=[iv])
    vs_batt = iv[1]

    vec_ln = [0] * 8
    act_xsq = [0] * 8
    act_rt = [0] * 8
    vs_xsqred = [0] * 8
    for tt in range(8):
        a_t = acc[:, tt * D:(tt + 1) * D]
        ck_x = ck_ld1 if tt < 4 else ck_xb
        ia = inc("ACT")
        wts = [("LDC", ck_x)]
        if tt >= 1:
            wts.append(("VS", vs_xsqred[tt - 1]))
        I("s", lambda a_t=a_t: nc.scalar.activation(out=xsq[:, :], in_=a_t, func=AF.Square),
          waits=wts, incs=[ia])
        act_xsq[tt] = ia[1]

        I("v", lambda a_t=a_t, ck_x=ck_x: nc.vector.reduce_sum(stats[:, 0:1], a_t, axis=AX.X),
          waits=[("LDC", ck_x)])
        iv = inc("VS")
        I("v", lambda: nc.vector.reduce_sum(stats[:, 1:2], xsq[:, :], axis=AX.X),
          waits=[("ACT", act_xsq[tt])], incs=[iv])
        g0 = iv[1]
        vs_xsqred[tt] = g0
        I("v", lambda: nc.vector.tensor_scalar_mul(stats[:, 2:3], stats[:, 0:1], 1.0 / D),
          waits=[("VS", g0)])
        I("v", lambda: nc.vector.tensor_scalar_mul(stats[:, 3:4], stats[:, 1:2], 1.0 / D))
        iv = inc("VS")
        I("v", lambda: nc.vector.tensor_mul(stats[:, 4:5], stats[:, 2:3], stats[:, 2:3]), incs=[iv])
        g1 = iv[1]
        iv = inc("VS")
        I("v", lambda: nc.vector.tensor_sub(stats[:, 5:6], stats[:, 3:4], stats[:, 4:5]),
          waits=[("VS", g1)], incs=[iv])
        g2 = iv[1]
        iv = inc("VS")
        wts = [("VS", g2)]
        if tt >= 1:
            wts.append(("ACT", act_rt[tt - 1]))
        I("v", lambda: nc.vector.tensor_scalar_add(stats[:, 6:7], stats[:, 5:6], EPS),
          waits=wts, incs=[iv])
        g3 = iv[1]
        iv = inc("VS")
        I("v", lambda: nc.vector.reciprocal(stats[:, 7:8], stats[:, 6:7]),
          waits=[("VS", g3)], incs=[iv])
        g4 = iv[1]
        iv = inc("VS")
        I("v", lambda: nc.vector.tensor_mul(stats[:, 8:9], stats[:, 5:6], stats[:, 7:8]),
          waits=[("VS", g4)], incs=[iv])
        g5 = iv[1]
        I("v", lambda tt=tt: nc.vector.tensor_scalar_mul(sq_ffn[:, tt:tt + 1], stats[:, 8:9], float(D)),
          waits=[("VS", g5)])
        ia = inc("ACT")
        I("s", lambda: nc.scalar.activation(out=stats[:, 9:10], in_=stats[:, 6:7], func=AF.Sqrt),
          waits=[("VS", g3)], incs=[ia])
        act_rt[tt] = ia[1]
        iv = inc("VS")
        I("v", lambda: nc.vector.reciprocal(stats[:, 10:11], stats[:, 9:10]),
          waits=[("ACT", act_rt[tt])], incs=[iv])
        g6 = iv[1]
        I("v", lambda: nc.vector.tensor_scalar_add(stats[:, 11:12], stats[:, 10:11], 1.0),
          waits=[("VS", g6)])
        iv = inc("VS")
        I("v", lambda: nc.vector.tensor_mul(stats[:, 12:13], stats[:, 2:3], stats[:, 10:11]), incs=[iv])
        g7 = iv[1]
        I("v", lambda tt=tt, a_t=a_t: nc.vector.tensor_scalar(
            xn_tm[:, tt * D:(tt + 1) * D], a_t, stats[:, 2:3], stats[:, 10:11], ALU.subtract, ALU.mult),
          waits=[("VS", g7)])
        ivv = inc("VEC")
        I("v", lambda a_t=a_t: nc.vector.tensor_scalar(
            a_t, a_t, stats[:, 11:12], stats[:, 12:13], ALU.mult, ALU.subtract), incs=[ivv])
        vec_ln[tt] = ivv[1]
    vec_ln_all = cnt["VEC"]

    ev_tr = {}
    for tt in range(8):
        for half in range(2):
            pbuf = pD_bf if half == 0 else pE_bf
            idx = tt * 2 + half
            for j in range(8):
                cc = half * 8 + j
                wts = []
                if j == 0:
                    wts = [("VEC", vec_ln[tt])]
                    if idx == 0:
                        wts.append(("LDC", ck_ld1))
                    if idx >= 2:
                        wts.append(("VEC", ev_tr[idx - 2]))
                im = inc("MM") if j == 7 else None
                I("t", lambda tt=tt, cc=cc, j=j, pbuf=pbuf: nc.tensor.transpose(
                    pbuf[:, j * 128:(j + 1) * 128],
                    xn_tm[:, tt * D + cc * 128: tt * D + cc * 128 + 128],
                    ident_sb[:, :]),
                  waits=wts, incs=[im])
                if im:
                    mm_tr = im[1]
            ivv = inc("VEC")

            def evac_fm(tt=tt, half=half, pbuf=pbuf):
                src = pbuf.rearrange("p (j f) -> p j f", j=8)
                dst = xn_fm.rearrange("p (j r) -> p j r", r=1040)[
                    :, half * 8:(half + 1) * 8, tt * 128:tt * 128 + 128]
                return nc.vector.tensor_copy(dst, src)
            I("v", evac_fm, waits=[("MM", mm_tr)], incs=[ivv])
            ev_tr[idx] = ivv[1]
    vec_fm_all = cnt["VEC"]


    ck_wt = [0] * NFT
    ck_ffnw = 0
    mm_qkv = [0] * NFT
    act_stage = [0] * NFT
    sc_contrib = [0] * NFT
    mm_norm = {}
    vs_norm = {}
    ck_q2r = {}
    psum_qkv = [pA, pB]
    norm_fts = []
    SCP = ("SCA", "SCB")
    last_q2r_ck = 0

    for ft in range(NFT):
        par = ft % 2
        src = w_attnT[:, 128 * ft:128 * (ft + 1)].rearrange("(cc p) f -> p cc f", p=128)
        wts = []
        if ft >= 2:
            wts.append(("MM", mm_qkv[ft - 2]))
        I("gp", lambda src=src, par=par: nc.gpsimd.dma_start(
            out=wt_buf[par].rearrange("p (cc f) -> p cc f", f=128), in_=src),
          waits=wts, incs=[inc("GP", 16)])
        ck_wt[ft] = CK("gp", "GP", "GPC")
        if ft == 0:
            ck_ffnw = ck_wt[0]

        if ft < 16:
            sec, h = 0, ft
        elif ft < 32:
            sec, h = 1, ft - 16
        else:
            sec, h = 2, ft - 32
        ps = psum_qkv[par]
        wts0 = [("GPC", ck_wt[ft]), ("VEC", vec_fm_all)]
        if ft >= 2:
            wts0.append(("ACT", act_stage[ft - 2]))
        for cc in range(NCC):
            lhsT = wt_buf[par][:, cc * 128:(cc + 1) * 128]
            rhs0 = xn_fm[:, cc * 1040: cc * 1040 + 512]
            rhs1 = xn_fm[:, cc * 1040 + 512: cc * 1040 + 1024]
            I("t", lambda ps=ps, lhsT=lhsT, rhs0=rhs0, cc=cc: nc.tensor.matmul(
                ps[:, 0:512], lhsT, rhs0, start=(cc == 0), stop=(cc == NCC - 1)),
              waits=wts0 if cc == 0 else [])
            im = inc("MM") if cc == NCC - 1 else None
            I("t", lambda ps=ps, lhsT=lhsT, rhs1=rhs1, cc=cc: nc.tensor.matmul(
                ps[:, 512:1024], lhsT, rhs1, start=(cc == 0), stop=(cc == NCC - 1)),
              incs=[im])
        mm_qkv[ft] = cnt["MM"]

        scale = INV_SQRT_HD if sec == 0 else 1.0
        ia = inc("ACT")
        wts = [("MM", mm_qkv[ft]), ("VS", vs_batt)]
        if ft >= 2:
            wts.append((SCP[par], sc_contrib[ft - 2]))
        I("s", lambda ps=ps, par=par, ft=ft, scale=scale: nc.scalar.activation(
            out=kqv_stage[par][:, :], in_=ps[:, :], func=AF.Identity,
            bias=b_att_sb[:, ft:ft + 1], scale=scale),
          waits=wts, incs=[ia])
        act_stage[ft] = ia[1]

        rowbase = SHROWS * (h // 2) + sec * 256 + (h % 2) * 128
        ii = inc(SCP[par], 16)
        I("s", lambda par=par, rowbase=rowbase: nc.scalar.dma_start(
            out=a2a_in[rowbase:rowbase + 128, :], in_=kqv_stage[par][:, :]),
          waits=[("ACT", act_stage[ft])], incs=[ii])
        sc_contrib[ft] = ii[1]

        if sec in (0, 1):
            nidx = len(norm_fts)
            ia = inc("ACT")
            wts = []
            if norm_fts:
                wts.append(("MM", mm_norm[norm_fts[-1]]))
            if ft == 0:
                wts.append(("VEC", vec_ln[6]))
            I("s", lambda par=par: nc.scalar.activation(
                out=ktmp[:, 0:1024], in_=kqv_stage[par][:, :], func=AF.Square),
              waits=wts, incs=[ia])
            act_kt = ia[1]
            wts = [("ACT", act_kt)]
            if norm_fts:
                wts.append(("VS", vs_norm[norm_fts[-1]]))
            else:
                wts.append(("GS", gs_ones))
            I("t", lambda: nc.tensor.matmul(pC[0:1, 0:512], ones_col_f[:, :], ktmp[:, 0:512],
                                            start=True, stop=True), waits=wts)
            im = inc("MM")
            I("t", lambda: nc.tensor.matmul(pC[0:1, 512:1024], ones_col_f[:, :], ktmp[:, 512:1024],
                                            start=True, stop=True), incs=[im])
            mm_norm[ft] = im[1]
            if sec == 0:
                iv2 = inc("VS")
                wts = [("MM", mm_norm[ft])]
                if last_q2r_ck:
                    wts.append(("LDC", last_q2r_ck))
                I("v", lambda: nc.vector.tensor_copy(qb_sb[0:1, :], pC[0:1, 0:1024]),
                  waits=wts, incs=[iv2])
                vs_norm[ft] = iv2[1]
                ia2 = inc("ACT")
                I("s", lambda: nc.scalar.activation(out=qb_sb[0:1, :], in_=qb_sb[0:1, :],
                                                    func=AF.Sqrt),
                  waits=[("VS", iv2[1])], incs=[ia2])
                rowq = SHROWS * (h // 2) + 768 + (h % 2)
                I("sync", lambda rowq=rowq: nc.sync.dma_start(
                    out=a2a_in[rowq:rowq + 1, :], in_=qb_sb[0:1, :]),
                  waits=[("ACT", ia2[1])], incs=[inc("LD", 16)])
                ck_q2r[ft] = CK("sync", "LD", "LDC")
                last_q2r_ck = ck_q2r[ft]
            else:
                s0 = RS_NTMP + 2 * nidx
                iv = inc("VS")
                I("v", lambda s0=s0: nc.vector.reduce_max(
                    rowscr[0:1, s0:s0 + 1], pC[0:1, 0:512], axis=AX.X),
                  waits=[("MM", mm_norm[ft])], incs=[iv])
                iv2 = inc("VS")
                I("v", lambda s0=s0: nc.vector.reduce_max(
                    rowscr[0:1, s0 + 1:s0 + 2], pC[0:1, 512:1024], axis=AX.X),
                  incs=[iv2])
                vs_norm[ft] = iv2[1]
            norm_fts.append(ft)
    act_qkv_all = cnt["ACT"]
    vs_norm_all = vs_norm[norm_fts[-1]]
    W("s", ("SCA", cnt["SCA"]), ("SCB", cnt["SCB"]))
    isc = inc("SCC")
    I("s", lambda: nc.scalar.nop(), incs=[isc])
    scc_contrib = isc[1]

    ivv = None
    kfts = [ft for ft in norm_fts if ft >= 16]
    for i, ft in enumerate(kfts):
        nidx = norm_fts.index(ft)
        h = ft - 16
        slot = RS_MAXK + h
        s0 = RS_NTMP + 2 * nidx
        ivv = inc("VEC") if i == len(kfts) - 1 else None
        I("v", lambda slot=slot, s0=s0: nc.vector.tensor_max(
            rowscr[0:1, slot:slot + 1], rowscr[0:1, s0:s0 + 1], rowscr[0:1, s0 + 1:s0 + 2]),
          waits=[("VS", vs_norm_all)] if i == 0 else [], incs=[ivv])
    vec_norm_all = ivv[1]

    iv = inc("VS")
    I("v", lambda: nc.vector.tensor_copy(m8_b[0:8, :], m8_f[0:8, :]),
      waits=[("LDC", ck_ld1)], incs=[iv])
    vs_m8b = iv[1]
    iv = inc("VS")
    I("v", lambda: nc.vector.tensor_mul(m8_f[0:8, :], m8_f[0:8, :], m8_f[0:8, :]),
      waits=[("VS", vs_m8b)], incs=[iv])
    vs_msq = iv[1]
    ivv = inc("VEC")
    I("v", lambda: nc.vector.reduce_sum(stats[0:8, 13:14], m8_f[0:8, :], axis=AX.X),
      waits=[("VS", vs_msq)], incs=[ivv])
    vec_mm = ivv[1]
    for n in range(8):
        I("sync", lambda n=n: nc.sync.dma_start(out=rowscr[0:1, RS_MMS + 8 * n:RS_MMS + 8 * n + 1],
                                                in_=stats[n:n + 1, 13:14],
                                                allow_slow_non_contiguous=True),
          waits=[("VEC", vec_mm)] if n == 0 else [], incs=[inc("LD", 16)])
    ck_mmrow = CK("sync", "LD", "LDC")
    iv = inc("VS")
    I("v", lambda: nc.vector.tensor_copy(
        rowscr[0:1, RS_MM:RS_MM + 8],
        rowscr[0:1, RS_MMS:RS_MMS + 64].rearrange("a (j q) -> a j q", q=8)[:, :, 0:1].squeeze(2)),
      waits=[("LDC", ck_mmrow)], incs=[iv])
    vs_mmrow = iv[1]

    vec_mtr = {}
    for half in range(2):
        for j in range(8):
            cc = half * 8 + j
            wts = []
            if cc == 0:
                wts = [("VS", vs_m8b), ("VEC", vec_fm_all)]
            elif j == 0:
                wts = [("VEC", vec_mtr[0])]
            im = inc("MM") if j == 7 else None
            I("t", lambda cc=cc, j=j: nc.tensor.transpose(
                pD_bf[:, j * 8:(j + 1) * 8],
                m8_b[0:8, cc * 128:(cc + 1) * 128], ident_sb[0:8, 0:8]),
              waits=wts, incs=[im])
            if im:
                mm_m = im[1]
        ivv = inc("VEC")

        def evac_m(half=half):
            src = pD_bf[:, 0:64].rearrange("p (j f) -> p j f", j=8)
            dst = xn_fm.rearrange("p (j r) -> p j r", r=1040)[
                :, half * 8:(half + 1) * 8, 1024:1032]
            return nc.vector.tensor_copy(dst, src)
        I("v", evac_m, waits=[("MM", mm_m)], incs=[ivv])
        vec_mtr[half] = ivv[1]

    I("sync", lambda: nc.sync.dma_start(out=m8_f[0:8, :], in_=debias8),
      waits=[("VEC", vec_mm)], incs=[inc("LD", 16)])
    ck_deb = CK("sync", "LD", "LDC")
    iv = inc("VS")
    I("v", lambda: nc.vector.tensor_copy(m8_b[0:8, :], m8_f[0:8, :]),
      waits=[("LDC", ck_deb), ("VEC", vec_mtr[1])], incs=[iv])
    vs_debb = iv[1]
    vec_dtr = {}
    for half in range(2):
        for j in range(8):
            cc = half * 8 + j
            wts = []
            if cc == 0:
                wts = [("VS", vs_debb)]
            elif j == 0:
                wts = [("VEC", vec_dtr[0])]
            im = inc("MM") if j == 7 else None
            I("t", lambda cc=cc, j=j: nc.tensor.transpose(
                pE_bf[:, j * 8:(j + 1) * 8],
                m8_b[0:8, cc * 128:(cc + 1) * 128], ident_sb[0:8, 0:8]),
              waits=wts, incs=[im])
            if im:
                mm_d = im[1]
        ivv = inc("VEC")

        def evac_d(half=half):
            src = pE_bf[:, 0:64].rearrange("p (j f) -> p j f", j=8)
            dst = xn_fm.rearrange("p (j r) -> p j r", r=1040)[
                :, half * 8:(half + 1) * 8, 1032:1040]
            return nc.vector.tensor_copy(dst, src)
        I("v", evac_d, waits=[("MM", mm_d)], incs=[ivv])
        vec_dtr[half] = ivv[1]
    vec_xm_all = cnt["VEC"]


    I("gp", lambda: nc.gpsimd.dma_start(out=dgnb_sb[0:72, :], in_=dgnb),
      waits=[("MM", mm_qkv[NFT - 2])], incs=[inc("GP", 16)])
    ck_dgnb = CK("gp", "GP", "GPC")

    mm_ffnA = [0] * 9
    vec_ffnA = [0] * 9
    for tile in range(9):
        par = tile % 2
        ps = psum_qkv[par]
        ncols = 128 if tile < 8 else 16
        wts0 = [("GPC", ck_ffnw), ("ACT", act_qkv_all)]
        if tile == 0:
            wts0.append(("VEC", vec_xm_all))
        if tile >= 2:
            wts0.append(("VEC", vec_ffnA[tile - 2]))
        for cc in range(NCC):
            colbase = cc * 1040 + tile * 128 if tile < 8 else cc * 1040 + 1024
            im = inc("MM") if cc == NCC - 1 else None
            I("t", lambda ps=ps, cc=cc, ncols=ncols, colbase=colbase: nc.tensor.matmul(
                ps[0:ncols, 0:136],
                xn_fm[:, colbase: colbase + ncols],
                ffn_w_sb[:, cc * 136:(cc + 1) * 136],
                start=(cc == 0), stop=(cc == NCC - 1)),
              waits=wts0 if cc == 0 else [], incs=[im])
        mm_ffnA[tile] = cnt["MM"]
        ivv = inc("VEC")
        if tile < 8:
            I("v", lambda ps=ps, tile=tile: nc.vector.tensor_copy(
                A_sb[:, tile * 136:(tile + 1) * 136], ps[0:128, 0:136]),
              waits=[("MM", mm_ffnA[tile])], incs=[ivv])
        else:
            I("v", lambda ps=ps: nc.vector.tensor_copy(cross[0:16, 0:136], ps[0:16, 0:136]),
              waits=[("MM", mm_ffnA[tile])], incs=[ivv])
        vec_ffnA[tile] = ivv[1]
    vec_cross = vec_ffnA[8]

    for n in range(8):
        I("gp", lambda n=n: nc.gpsimd.dma_start(
            out=rowscr[0:1, RS_CROW + 8 * n: RS_CROW + 8 * n + 8],
            in_=cross[n:n + 1, 8 * n:8 * n + 8], allow_slow_non_contiguous=True),
          waits=[("VEC", vec_cross)] if n == 0 else [], incs=[inc("GP", 16)])
        I("gp", lambda n=n: nc.gpsimd.dma_start(
            out=rowscr[0:1, RS_CUROW + 8 * n: RS_CUROW + 8 * n + 8],
            in_=cross[8 + n:9 + n, 72 + 8 * n:72 + 8 * n + 8], allow_slow_non_contiguous=True),
          incs=[inc("GP", 16)])
    ck_rows = CK("gp", "GP", "GPC")

    ia2 = inc("ACT")
    I("s", lambda: nc.scalar.activation(out=rowscr[0:1, RS_MAXK:RS_MAXK + 16],
                                        in_=rowscr[0:1, RS_MAXK:RS_MAXK + 16],
                                        func=AF.Sqrt),
      waits=[("VEC", vec_norm_all)], incs=[ia2])
    ivv = inc("VEC")
    I("v", lambda: nc.vector.tensor_copy(
        aux_bf[0:1, 0:16], rowscr[0:1, RS_MAXK:RS_MAXK + 16]),
      waits=[("ACT", ia2[1])], incs=[ivv])
    vec_aux = ivv[1]
    for j in range(NCORES):
        row = SHROWS * j + 770
        I("sync", lambda j=j, row=row: nc.sync.dma_start(
            out=a2a_in[row:row + 1, 0:2], in_=aux_bf[0:1, 2 * j:2 * j + 2],
            allow_slow_non_contiguous=True),
          waits=[("VEC", vec_aux)] if j == 0 else [], incs=[inc("LD", 16)])
    ck_aux = CK("sync", "LD", "LDC")

    ic = inc("CC")
    I("gp", lambda: nc.gpsimd.collective_compute(
        "AllGather", ALU.bypass, replica_groups=[list(range(NCORES))],
        ins=[a2a_in[:, :]], outs=[a2a_out[:, :]]),
      waits=[("LDC", ck_aux), ("SCC", scc_contrib)], incs=[ic])
    ic = inc("CC")
    I("gp", lambda: nc.gpsimd.collective_compute(
        "AllGather", ALU.bypass, replica_groups=[list(range(NCORES))],
        ins=[fence_in[:, :]], outs=[fence_out[:, :]]),
      waits=[("CC", ic[1] - 1)], incs=[ic])
    cc_qkv = ic[1]

    bclist = [
        (c_bc[:, :], rowscr[0:1, RS_CROW:RS_CROW + 64], 64, [("GPC", ck_rows)]),
        (cu_bc[:, :], rowscr[0:1, RS_CUROW:RS_CUROW + 64], 64, []),
        (mm_bcv[:, :], rowscr[0:1, RS_MM:RS_MM + 8], 8, [("VS", vs_mmrow)]),
        (cen_bc[:, :], misc_sb[0:1, 0:8], 8, [("LDC", ck_ld1)]),
        (slo_bc[:, :], misc_sb[0:1, 8:16], 8, []),
    ]
    vec_bcs = {}
    for i, (dst, srcrow, n, extra) in enumerate(bclist):
        off = 0 if i % 2 == 0 else 512
        wts = list(extra) + [("GS", gs_ones), ("VEC", vec_norm_all)]
        if i >= 2:
            wts.append(("VEC", vec_bcs[i - 2]))
        im = inc("MM")
        I("t", lambda srcrow=srcrow, n=n, off=off: nc.tensor.matmul(
            pC[:, off:off + n], ones_row_f[:, :], srcrow, start=True, stop=True),
          waits=wts, incs=[im])
        ivv = inc("VEC")
        I("v", lambda dst=dst, n=n, off=off: nc.vector.tensor_copy(dst, pC[:, off:off + n]),
          waits=[("MM", im[1])], incs=[ivv])
        vec_bcs[i] = ivv[1]
    prev_bp_mm = 0
    for g in range(4):
        wts = [] if g == 0 else [("MM", prev_bp_mm)]
        I("sync", lambda g=g: nc.sync.dma_start(out=rowscr[0:1, 0:512],
                                                in_=b_proj_in[0:1, 512 * g:512 * (g + 1)]),
          waits=wts, incs=[inc("LD", 16)])
        ck_bp = CK("sync", "LD", "LDC")
        im = inc("MM")
        wts = [("LDC", ck_bp), ("GS", gs_ones)]
        wts.append(("VEC", vec_bcs[4] if g == 0 else vec_bcs[len(bclist) + g - 1]))
        I("t", lambda: nc.tensor.matmul(pC[:, 0:512], ones_row_f[:, :], rowscr[0:1, 0:512],
                                        start=True, stop=True),
          waits=wts, incs=[im])
        prev_bp_mm = im[1]
        ivv = inc("VEC")
        I("v", lambda g=g: nc.vector.tensor_copy(bproj_bc[:, 512 * g:512 * (g + 1)], pC[:, 0:512]),
          waits=[("MM", im[1])], incs=[ivv])
        vec_bcs[len(bclist) + g] = ivv[1]
    vec_bc_all = cnt["VEC"]

    vec_ffn_done = [0] * 8
    vec_utr_prev = 0
    for tt in range(8):
        At = A_sb[:, tt * 136:(tt + 1) * 136]
        iv = inc("VS")
        I("v", lambda At=At: nc.vector.tensor_sub(ffn_tmp[:, 0:64], At[:, 0:64], c_bc[:, :]),
          waits=[("VEC", vec_bc_all), ("VEC", vec_ffnA[tt])], incs=[iv])
        g0 = iv[1]
        iv = inc("VS")
        I("v", lambda: nc.vector.tensor_mul(ffn_tmp[:, 64:128], ffn_tmp[:, 0:64], ffn_tmp[:, 0:64]),
          waits=[("VS", g0)], incs=[iv])
        g1 = iv[1]
        for n in range(8):
            iv = inc("VS") if n == 7 else None
            I("v", lambda n=n: nc.vector.reduce_sum(
                ffn_tmp[:, 128 + n:129 + n], ffn_tmp[:, 64 + 8 * n:64 + 8 * n + 8], axis=AX.X),
              waits=[("VS", g1)] if n == 0 else [], incs=[iv])
        g2 = cnt["VS"]
        iv = inc("VS")
        I("v", lambda At=At, tt=tt: nc.vector.tensor_scalar(
            ffn_tmp[:, 136:144], At[:, 64:72], -2.0, sq_ffn[:, tt:tt + 1], ALU.mult, ALU.add),
          waits=[("VS", g2)], incs=[iv])
        g3 = iv[1]
        iv = inc("VS")
        I("v", lambda: nc.vector.tensor_add(ffn_tmp[:, 144:152], ffn_tmp[:, 136:144], mm_bcv[:, :]),
          waits=[("VS", g3)], incs=[iv])
        g4 = iv[1]
        iv = inc("VS")
        I("v", lambda: nc.vector.reciprocal(ffn_tmp[:, 152:160], ffn_tmp[:, 144:152]),
          waits=[("VS", g4)], incs=[iv])
        g5 = iv[1]
        iv = inc("VS")
        I("v", lambda: nc.vector.tensor_mul(ffn_tmp[:, 160:168], ffn_tmp[:, 128:136], ffn_tmp[:, 152:160]),
          waits=[("VS", g5)], incs=[iv])
        g6 = iv[1]
        iv = inc("VS")
        I("v", lambda: nc.vector.tensor_sub(ffn_tmp[:, 168:176], ffn_tmp[:, 160:168], cen_bc[:, :]),
          waits=[("VS", g6)], incs=[iv])
        g7 = iv[1]
        iv = inc("VS")
        I("v", lambda: nc.vector.tensor_mul(ffn_tmp[:, 176:184], ffn_tmp[:, 168:176], slo_bc[:, :]),
          waits=[("VS", g7)], incs=[iv])
        g8 = iv[1]
        ia = inc("ACT")
        I("s", lambda: nc.scalar.activation(out=ffn_tmp[:, 184:192], in_=ffn_tmp[:, 176:184], func=AF.Sigmoid),
          waits=[("VS", g8)], incs=[ia])
        act_gate = ia[1]
        iv = inc("VS")
        I("v", lambda tt=tt: nc.vector.reduce_max(gmax_all[:, tt:tt + 1], ffn_tmp[:, 184:192], axis=AX.X),
          waits=[("ACT", act_gate)], incs=[iv])
        g9 = iv[1]
        iv = inc("VS")
        I("v", lambda tt=tt: nc.vector.tensor_scalar(
            ffn_tmp[:, 192:200], ffn_tmp[:, 184:192], gmax_all[:, tt:tt + 1], None, ALU.is_ge),
          waits=[("VS", g9)], incs=[iv])
        g10 = iv[1]
        iv = inc("VS")
        I("v", lambda: nc.vector.tensor_scalar_mul(ffn_tmp[:, 200:201], ffn_tmp[:, 192:193], 0.0),
          waits=[("VS", g10)], incs=[iv])
        gpx = iv[1]
        for n in range(1, 8):
            iv = inc("VS")
            I("v", lambda n=n: nc.vector.tensor_add(
                ffn_tmp[:, 200 + n:201 + n], ffn_tmp[:, 199 + n:200 + n], ffn_tmp[:, 191 + n:192 + n]),
              waits=[("VS", gpx)], incs=[iv])
            gpx = iv[1]
        g11 = cnt["VS"]
        iv = inc("VS")
        I("v", lambda: nc.vector.tensor_scalar(
            ffn_tmp[:, 208:216], ffn_tmp[:, 200:208], 0.0, None, ALU.is_equal),
          waits=[("VS", g11)], incs=[iv])
        g12 = iv[1]
        iv = inc("VS")
        I("v", lambda: nc.vector.tensor_mul(ffn_tmp[:, 216:224], ffn_tmp[:, 192:200], ffn_tmp[:, 208:216]),
          waits=[("VS", g12)], incs=[iv])
        g13 = iv[1]
        iv = inc("VS")
        I("v", lambda At=At: nc.vector.tensor_sub(ffn_tmp[:, 0:64], At[:, 72:136], cu_bc[:, :]),
          waits=[("VS", g13)], incs=[iv])
        g14 = iv[1]
        for n in range(8):
            iv = inc("VS") if n == 7 else None
            I("v", lambda n=n: nc.vector.tensor_scalar(
                umt[:, 8 * n:8 * n + 8], ffn_tmp[:, 8 * n:8 * n + 8],
                ffn_tmp[:, 216 + n:217 + n], None, ALU.mult),
              waits=[("VS", g14)] if n == 0 else [], incs=[iv])
        g15 = cnt["VS"]
        ivv = inc("VEC")
        I("v", lambda: nc.vector.tensor_copy(umt[:, 64:72], ffn_tmp[:, 216:224]),
          waits=[("VS", g15)], incs=[ivv])
        vec_umt = ivv[1]
        im = inc("MM")
        wts = [("VEC", vec_umt)]
        wts.append(("VEC", vec_utr_prev if tt >= 1 else vec_mtr[1]))
        I("t", lambda: nc.tensor.transpose(pD_bf[0:72, 0:128], umt[:, 0:72], ident_sb[:, :]),
          waits=wts, incs=[im])
        ivv = inc("VEC")
        I("v", lambda tt=tt: nc.vector.tensor_copy(u_fm[0:72, tt * 128:(tt + 1) * 128], pD_bf[0:72, 0:128]),
          waits=[("MM", im[1])], incs=[ivv])
        vec_utr_prev = ivv[1]
        mm_mod = {}
        for fg in range(4):
            ps = [pA[:, 0:512], pA[:, 512:1024], pB[:, 0:512], pB[:, 512:1024]][fg]
            wts = [("VEC", vec_utr_prev), ("GPC", ck_dgnb)]
            wts.append(("VEC", vec_ffn_done[tt - 1] if tt >= 1 else vec_ffnA[8]))
            im = inc("MM")
            I("t", lambda ps=ps, tt=tt, fg=fg: nc.tensor.matmul(
                ps, u_fm[0:72, tt * 128:(tt + 1) * 128],
                dgnb_sb[0:72, 512 * fg:512 * (fg + 1)], start=True, stop=True),
              waits=wts, incs=[im])
            mm_mod[fg] = im[1]
        for fg in range(4):
            ps = [pA[:, 0:512], pA[:, 512:1024], pB[:, 0:512], pB[:, 512:1024]][fg]
            xnsl = xn_tm[:, tt * D + 512 * fg: tt * D + 512 * (fg + 1)]
            asl = acc[:, tt * D + 512 * fg: tt * D + 512 * (fg + 1)]
            iv = inc("VS")
            I("v", lambda ps=ps, xnsl=xnsl: nc.vector.tensor_sub(ffn_tmp[:, 0:512], ps, xnsl),
              waits=[("MM", mm_mod[fg])], incs=[iv])
            ivv = inc("VEC")
            I("v", lambda asl=asl, tt=tt: nc.vector.scalar_tensor_tensor(
                asl, ffn_tmp[:, 0:512], gmax_all[:, tt:tt + 1], asl, ALU.mult, ALU.add),
              waits=[("VS", iv[1])], incs=[ivv])
        ivv = inc("VEC")
        I("v", lambda tt=tt: nc.vector.tensor_add(
            acc[:, tt * D:(tt + 1) * D], acc[:, tt * D:(tt + 1) * D], bproj_bc[:, :]),
          incs=[ivv])
        vec_ffn_done[tt] = ivv[1]
    vec_ffn_all = cnt["VEC"]
    vec_utr_last = vec_utr_prev

    def pid_init():
        pid_holder["sync"] = nc.partition_id(engines=[mybir.EngineType.SP])
        return nc.sync.nop()
    I("sync", pid_init)

    def qkv_slab():
        pid = pid_holder["sync"]
        v = a2a_out.rearrange("(i s r) t -> i s r t", s=NCORES, r=SHROWS)
        src_ = v[0:NCORES, bass.ts(pid, 1), :, :].squeeze(1)
        dst_ = my_qkv.rearrange("(i r) t -> i r t", r=SHROWS)
        return nc.sync.dma_start(out=dst_, in_=src_)
    I("sync", qkv_slab, waits=[("CC", cc_qkv)], incs=[inc("LD", 16)])
    ck_slab = CK("sync", "LD", "LDC")

    a2v = my_qkv.rearrange("(i r) t -> i r t", r=SHROWS)

    def a2v_my(i0, i1, r0, r1):
        return a2v[i0:i1, r0:r1, :].unsqueeze(1)

    for j in range(NCORES):
        I("sync", lambda j=j: nc.sync.dma_start(
            out=rowscr[0:1, RS_AUXB + 8 * j:RS_AUXB + 8 * j + 1].bitcast(BF16),
            in_=a2v_my(j, j + 1, 770, 771)[:, :, :, 0:2].squeeze(2).squeeze(1),
            allow_slow_non_contiguous=True),
          incs=[inc("LD", 16)])
    ck_auxout = CK("sync", "LD", "LDC")
    iv = inc("VS")
    I("v", lambda: nc.vector.tensor_copy(
        rowscr[0:1, RS_AUXF:RS_AUXF + 16].rearrange("a (j h) -> a j h", h=2),
        rowscr[0:1, RS_AUXB:RS_AUXB + 64].bitcast(BF16).rearrange(
            "a (j q) -> a j q", q=16)[:, :, 0:2]),
      waits=[("LDC", ck_auxout)], incs=[iv])
    g0 = iv[1]
    AF_ = RS_AUXF
    PM = RS_T0
    iv = inc("VS")
    I("v", lambda: nc.vector.tensor_copy(rowscr[0:1, PM:PM + 2], rowscr[0:1, AF_:AF_ + 2]),
      waits=[("VS", g0)], incs=[iv])
    gp_ = iv[1]
    for j in range(1, NCORES):
        iv = inc("VS")
        I("v", lambda j=j: nc.vector.tensor_max(
            rowscr[0:1, PM + 2 * j:PM + 2 * j + 2],
            rowscr[0:1, PM + 2 * (j - 1):PM + 2 * j],
            rowscr[0:1, AF_ + 2 * j:AF_ + 2 * j + 2]),
          waits=[("VS", gp_)], incs=[iv])
        gp_ = iv[1]
    ivv = inc("VEC")
    I("v", lambda: nc.vector.tensor_scalar_mul(rowscr[0:1, RS_NM:RS_NM + 16],
                                               rowscr[0:1, PM:PM + 16], -1.0),
      waits=[("VS", gp_)], incs=[ivv])
    vec_negm = ivv[1]

    vstage = [p_sb[:, 0:1024], bc_sb.bitcast(BF16)[:, 0:1024]]
    ck_vst = {}
    mm_vtr = {}
    vec_vtr = {}
    for hl in range(2):
        for j in range(NCORES):
            k = hl * 8 + j
            par = k % 2
            wts = []
            if k >= 2:
                wts.append(("MM", mm_vtr[k - 2]))
            I("sync", lambda hl=hl, j=j, par=par: nc.sync.dma_start(
                out=vstage[par][:, :],
                in_=a2v_my(j, j + 1, 512 + hl * 128, 512 + (hl + 1) * 128).squeeze(1).squeeze(0)),
              waits=wts, incs=[inc("LD", 16)])
            ck_vst[k] = CK("sync", "LD", "LDC")
            pbuf = pD_bf if par == 0 else pE_bf
            for ttv in range(8):
                wts = []
                if ttv == 0:
                    wts = [("LDC", ck_vst[k])]
                    if k == 0:
                        wts.append(("VEC", vec_utr_last))
                    elif k == 1:
                        wts.append(("VEC", vec_xm_all))
                    else:
                        wts.append(("VEC", vec_vtr[k - 2]))
                im = inc("MM") if ttv == 7 else None
                I("t", lambda par=par, ttv=ttv, pbuf=pbuf: nc.tensor.transpose(
                    pbuf[:, ttv * 128:(ttv + 1) * 128], vstage[par][:, ttv * 128:(ttv + 1) * 128],
                    ident_sb[:, :]),
                  waits=wts, incs=[im])
            mm_vtr[k] = cnt["MM"]
            ivv = inc("VEC")
            I("v", lambda hl=hl, j=j, pbuf=pbuf: nc.vector.tensor_copy(
                V_tm[:, hl * 8192 + j * 1024: hl * 8192 + (j + 1) * 1024], pbuf[:, 0:1024]),
              waits=[("MM", mm_vtr[k])], incs=[ivv])
            vec_vtr[k] = ivv[1]
    vec_vtr_all = cnt["VEC"]

    for hl in range(2):
        wts = [("VEC", vec_ffn_all), ("SCC", scc_contrib)] if hl == 0 else []
        I("sync", lambda hl=hl: nc.sync.dma_start(
            out=K_sb.rearrange("p (h j t) -> p h j t", h=2, j=8)[:, hl:hl + 1, :, :],
            in_=a2v_my(0, NCORES, 256 + hl * 128, 256 + (hl + 1) * 128).squeeze(1)
                .transpose([1, 0, 2]).unsqueeze(1)),
          waits=wts, incs=[inc("LD", 16)])
        I("sync", lambda hl=hl: nc.sync.dma_start(
            out=Q_sb.rearrange("p (h j t) -> p h j t", h=2, j=8)[:, hl:hl + 1, :, :],
            in_=a2v_my(0, NCORES, hl * 128, (hl + 1) * 128).squeeze(1)
                .transpose([1, 0, 2]).unsqueeze(1)),
          incs=[inc("LD", 16)])
    ck_kq = CK("sync", "LD", "LDC")

    def q2p_dma(m, waits=()):
        hl_, b_ = m // NQB, m % NQB
        rowq = SHROWS * (b_ // 2) + 768 + hl_
        c0 = (b_ % 2) * 512
        ii = inc(SCP[m % 2], 16)
        I("s", lambda rowq=rowq, c0=c0: nc.scalar.dma_start(
            out=qb_sb[0:1, 0:512], in_=my_qkv[rowq:rowq + 1, c0:c0 + 512]),
          waits=list(waits), incs=[ii])
        return ii[1]

    sp = [pA[:, 0:512], pA[:, 512:1024], pD[:, 0:512]]
    obank = [pB[:, 0:512], pC[:, 0:512]]
    sbank = [pB[0:1, 512:1024], pC[0:1, 512:1024]]
    bbc_sb = wp_buf.bitcast(F32)
    mm_kq_of = {}
    vec_badd_of = {}
    act_exp_of = {}
    mm_cons_of = {}
    vec_yev = {}
    vec_bcev = 0
    vec_bbc = 0
    pd_guard = 0
    chunk_ctr = 0
    prev_blk_mm = 0
    sc_q2p = {0: q2p_dma(0, waits=[("LDC", ck_slab)])}
    for hl in range(2):
        for b in range(NQB):
            m = hl * NQB + b
            pbk = b % 2
            nchunks = 4 * b + 4
            isc = inc("SCC")
            I("s", lambda: nc.scalar.nop(), waits=[(SCP[m % 2], sc_q2p[m])], incs=[isc])
            scc_q2p = isc[1]
            iv = inc("VS")
            nmcol = RS_NM + 2 * (b // 2) + hl
            wts = [("SCC", scc_q2p)]
            if prev_blk_mm:
                wts.append(("MM", prev_blk_mm))
            if m == 0:
                wts.append(("VEC", vec_negm))
            I("v", lambda nmcol=nmcol: nc.vector.tensor_scalar(
                qb_sb[0:1, 512:1024], qb_sb[0:1, 0:512],
                rowscr[0:1, nmcol:nmcol + 1], SHIFT, ALU.mult, ALU.add),
              waits=wts, incs=[iv])
            vs_B = iv[1]
            if m + 1 < 2 * NQB:
                sc_q2p[m + 1] = q2p_dma(m + 1, waits=[("VS", vs_B)])
            im = inc("MM")
            wts = [("VS", vs_B)]
            if m == 0:
                wts += [("LDC", ck_kq), ("VEC", vec_vtr_all)]
            else:
                wts.append(("VEC", vec_bbc))
            I("t", lambda: nc.tensor.matmul(pE[:, 0:512], ones_row_bf[:, :],
                                            qb_sb[0:1, 512:1024], start=True, stop=True),
              waits=wts, incs=[im])
            mm_bbc = im[1]
            ivv = inc("VEC")
            I("v", lambda: nc.vector.tensor_copy(bbc_sb[:, :], pE[:, 0:512]),
              waits=[("MM", mm_bbc)], incs=[ivv])
            vec_bbc = ivv[1]
            def emit_pv(gi, kcl, hl=hl, b=b, pbk=pbk, m=m, nchunks=nchunks):
                j0 = max(0, 128 * kcl - 512 * b)
                first = (kcl == 0)
                last = (kcl == nchunks - 1)
                p2 = gi % 2
                wts = [("ACT", act_exp_of[gi])]
                if first and m >= 2:
                    wts.append(("VEC", vec_yev[m - 2]))
                I("t", lambda: nc.tensor.matmul(
                    obank[pbk][:, j0:512],
                    V_tm[:, hl * 8192 + kcl * 128: hl * 8192 + (kcl + 1) * 128],
                    p_sb[:, p2 * 512 + j0: p2 * 512 + 512],
                    start=first, stop=last), waits=wts)
                im = inc("MM")
                I("t", lambda: nc.tensor.matmul(
                    sbank[pbk][:, j0:512], ones_col_bf[:, :],
                    p_sb[:, p2 * 512 + j0: p2 * 512 + 512],
                    start=first, stop=last), incs=[im])
                mm_cons_of[gi] = im[1]

            g0blk = chunk_ctr
            for kc in range(nchunks):
                gi = chunk_ctr
                chunk_ctr += 1
                s3 = gi % 3
                p2 = gi % 2
                j0 = max(0, 128 * kc - 512 * b)
                diag = (kc >= 4 * b)
                wts = []
                if gi >= 3:
                    wts.append(("ACT", act_exp_of[gi - 3]))
                else:
                    wts.append(("VEC", vec_vtr_all))
                if s3 == 2 and pd_guard:
                    wts.append(("VEC", pd_guard))
                    pd_guard = 0
                im = inc("MM")
                I("t", lambda hl=hl, b=b, kc=kc, s3=s3, j0=j0: nc.tensor.matmul(
                    sp[s3][:, j0:512],
                    K_sb[:, hl * 8192 + kc * 128: hl * 8192 + (kc + 1) * 128],
                    Q_sb[:, hl * 8192 + 512 * b + j0: hl * 8192 + 512 * (b + 1)],
                    start=True, stop=True), waits=wts, incs=[im])
                mm_kq_of[gi] = im[1]
                ivv = inc("VEC")
                wts = [("MM", mm_kq_of[gi])]
                if kc == 0:
                    wts.append(("VEC", vec_bbc))
                I("v", lambda s3=s3, j0=j0: nc.vector.tensor_add(
                    sp[s3][:, j0:512], sp[s3][:, j0:512], bbc_sb[:, j0:512]),
                  waits=wts, incs=[ivv])
                if diag:
                    ivv = inc("VEC")
                    I("v", lambda s3=s3, j0=j0: nc.vector.tensor_add(
                        sp[s3][:, j0:j0 + 128], sp[s3][:, j0:j0 + 128], causal_sb[:, :]),
                      incs=[ivv])
                vec_badd_of[gi] = ivv[1]
                if kc >= 2:
                    emit_pv(g0blk + kc - 2, kc - 2)
                wts = [("VEC", vec_badd_of[gi])]
                if gi >= 2:
                    wts.append(("MM", mm_cons_of[gi - 2]))
                ia = inc("ACT")
                I("s", lambda s3=s3, p2=p2, j0=j0: nc.scalar.activation(
                    out=p_sb[:, p2 * 512 + j0: p2 * 512 + 512], in_=sp[s3][:, j0:512],
                    func=AF.Exp),
                  waits=wts, incs=[ia])
                act_exp_of[gi] = ia[1]
            emit_pv(g0blk + nchunks - 2, nchunks - 2)
            emit_pv(g0blk + nchunks - 1, nchunks - 1)
            mm_last = mm_cons_of[chunk_ctr - 1]
            prev_blk_mm = mm_last
            iv = inc("VS")
            I("v", lambda pbk=pbk: nc.vector.reciprocal(rowscr[0:1, 0:512], sbank[pbk]),
              waits=[("MM", mm_last)], incs=[iv])
            g_r = iv[1]
            im = inc("MM")
            wts = [("VS", g_r), ("ACT", act_exp_of[chunk_ctr - 1])]
            if vec_bcev:
                wts.append(("VEC", vec_bcev))
            I("t", lambda: nc.tensor.matmul(pD[:, 0:512], ones_row_f[:, :], rowscr[0:1, 0:512],
                                            start=True, stop=True), waits=wts, incs=[im])
            ivv = inc("VEC")
            I("v", lambda: nc.vector.tensor_copy(bc_sb[:, :], pD[:, 0:512]),
              waits=[("MM", im[1])], incs=[ivv])
            vec_bcev = ivv[1]
            pd_guard = vec_bcev
            ivv = inc("VEC")
            I("v", lambda hl=hl, b=b, pbk=pbk: nc.vector.tensor_mul(
                y_sb[:, hl * 8192 + 512 * b: hl * 8192 + 512 * (b + 1)], obank[pbk], bc_sb[:, :]),
              waits=[("VEC", vec_bcev)], incs=[ivv])
            vec_yev[m] = ivv[1]
    vec_y_all = cnt["VEC"]
    act_exp_all = act_exp_of[chunk_ctr - 1]

    for hl in range(2):
        I("sync", lambda hl=hl: nc.sync.dma_start(
            out=y_in.rearrange("(j l p) t -> p l j t", l=2, p=128)[:, hl:hl + 1, :, :],
            in_=y_sb.rearrange("p (h j t) -> p h j t", h=2, j=8)[:, hl:hl + 1, :, :]),
          waits=[("VEC", vec_y_all)] if hl == 0 else [], incs=[inc("LD", 16)])
    ck_y = CK("sync", "LD", "LDC")
    ic = inc("CC")
    I("gp", lambda: nc.gpsimd.collective_compute(
        "AllGather", ALU.bypass, replica_groups=[list(range(NCORES))],
        ins=[y_in[:, :]], outs=[y_out[:, :]]),
      waits=[("LDC", ck_y)], incs=[ic])
    ic = inc("CC")
    I("gp", lambda: nc.gpsimd.collective_compute(
        "AllGather", ALU.bypass, replica_groups=[list(range(NCORES))],
        ins=[fence_in[:, :]], outs=[fence_out[:, :]]),
      waits=[("CC", ic[1] - 1)], incs=[ic])
    cc_y = ic[1]

    ck_yown = [0] * 16

    def yown_src(fi):
        pid = pid_holder["sync"]
        v = y_out.rearrange("(i s l p) t -> i s l p t", s=NCORES, l=2, p=128)
        return v[fi // 2:fi // 2 + 1, bass.ts(pid, 1), fi % 2:fi % 2 + 1, :, :] \
            .squeeze(2).squeeze(1).squeeze(0)

    SYP = ("SYA", "SYB")
    for fi in range(16):
        I("sync", lambda fi=fi: nc.sync.dma_start(
            out=yown[:, fi * 1024:(fi + 1) * 1024], in_=yown_src(fi)),
          waits=[("CC", cc_y)] if fi == 0 else [], incs=[inc(SYP[fi % 2], 16)])
        if fi >= 1:
            ck_yown[fi - 1] = CK("sync", SYP[(fi - 1) % 2], "LDC")
    ck_yown[15] = CK("sync", SYP[1], "LDC")

    qbanks = [pA[:, 0:512], pA[:, 512:1024], pB[:, 0:512], pB[:, 512:1024],
              pC[:, 0:512], pC[:, 512:1024], pD[:, 0:512], pE[:, 0:512]]
    mm_wp_cons = {}
    vec_oev = {}
    ck_out = {}
    ck_wp_of = {}
    GPP = ("GPA", "GPB")
    gpc0 = cnt["GPC"]
    for g in range(4):
        for fi in range(16):
            k = g * 16 + fi
            par = k % 2
            wts = []
            if k >= 2:
                wts.append(("MM", mm_wp_cons[k - 2]))
            I("gp", lambda g=g, fi=fi, par=par: nc.gpsimd.dma_start(
                out=wp_buf[:, par * 512:(par + 1) * 512],
                in_=w_projT[fi * 128:(fi + 1) * 128, g * 512:(g + 1) * 512]),
              waits=wts, incs=[inc(GPP[par], 16)])
            if k >= 1:
                ck_wp_of[k - 1] = CK("gp", GPP[(k - 1) % 2], "GPC")
            if k == 63:
                ck_wp_of[63] = CK("gp", GPP[1], "GPC")
            for qt in range(8):
                wts = []
                if qt == 0:
                    wts += [("GPC", gpc0 + k + 1), ("LDC", ck_yown[fi])]
                if fi == 0:
                    if g == 0:
                        if qt == 0:
                            wts += [("VEC", vec_y_all), ("ACT", act_exp_all), ("VEC", vec_bcev)]
                    else:
                        wts.append(("VEC", vec_oev[(g - 1, qt)]))
                im = inc("MM") if qt == 7 else None
                I("t", lambda fi=fi, qt=qt, par=par: nc.tensor.matmul(
                    qbanks[qt], yown[:, fi * 1024 + qt * 128: fi * 1024 + (qt + 1) * 128],
                    wp_buf[:, par * 512:(par + 1) * 512],
                    start=(fi == 0), stop=(fi == 15)),
                  waits=wts, incs=[im])
            mm_wp_cons[k] = cnt["MM"]
        for qt in range(8):
            k = g * 8 + qt
            par = k % 2
            wts = [("MM", mm_wp_cons[g * 16 + 15])]
            if k >= 2:
                wts.append(("LDC", ck_out[k - 2]))
            ivv = inc("VEC")
            I("v", lambda g=g, qt=qt, par=par: nc.vector.tensor_add(
                out_stage[par][:, 0:512], qbanks[qt],
                acc[:, qt * D + 512 * g: qt * D + 512 * (g + 1)]),
              waits=wts, incs=[ivv])
            vec_oev[(g, qt)] = ivv[1]
            I("sync", lambda g=g, qt=qt, par=par: nc.sync.dma_start(
                out=out_own[qt * 128:(qt + 1) * 128, g * 512:(g + 1) * 512],
                in_=out_stage[par][:, 0:512]),
              waits=[("VEC", vec_oev[(g, qt)])], incs=[inc("LD", 16)])
            ck_out[k] = CK("sync", "LD", "LDC")
    W("sync", ("LD", cnt["LD"]))

    with nc.Block() as block:
        @block.sync
        def _(eng):
            engmap["sync"] = nc.sync
            for fn in prog["sync"]:
                fn()

        @block.gpsimd
        def _(eng):
            engmap["gp"] = nc.gpsimd
            for fn in prog["gp"]:
                fn()

        @block.tensor
        def _(eng):
            engmap["t"] = nc.tensor
            for fn in prog["t"]:
                fn()

        @block.vector
        def _(eng):
            engmap["v"] = nc.vector
            for fn in prog["v"]:
                fn()

        @block.scalar
        def _(eng):
            engmap["s"] = nc.scalar
            for fn in prog["s"]:
                fn()

    return nc


_NC_CACHE = [None]


def kernel(L0_attn_x, L0_attn_w_attn, L0_attn_b_attn, L0_attn_w_proj, L0_attn_b_proj,
           L0_ffn_select_weight, L0_ffn_select_mean_diff, L0_ffn_imp_center,
           L0_ffn_imp_slope, L0_ffn_lora_update, L0_ffn_lora_degen,
           L0_ffn_bias_w, L0_ffn_debias_w):
    bf = ml_dtypes.bfloat16
    x2d = np.ascontiguousarray(np.asarray(L0_attn_x, dtype=np.float32).reshape(T, D))
    w_attnT = np.ascontiguousarray(np.asarray(L0_attn_w_attn, dtype=np.float32).T)
    b_att_st = np.ascontiguousarray(np.asarray(L0_attn_b_attn, dtype=np.float32).reshape(NFT, 128).T)
    w_projT = np.ascontiguousarray(np.asarray(L0_attn_w_proj, dtype=np.float32).T)
    b_proj_in = np.ascontiguousarray(np.asarray(L0_attn_b_proj, dtype=np.float32).reshape(1, D))
    sw = np.asarray(L0_ffn_select_weight, dtype=np.float32)
    md = np.asarray(L0_ffn_select_mean_diff, dtype=np.float32)
    lu = np.asarray(L0_ffn_lora_update, dtype=np.float32)
    ldg = np.asarray(L0_ffn_lora_degen, dtype=np.float32)
    bw = np.asarray(L0_ffn_bias_w, dtype=np.float32)
    dw = np.asarray(L0_ffn_debias_w, dtype=np.float32)
    ffn_w = np.ascontiguousarray(np.concatenate(
        [sw.transpose(1, 0, 2).reshape(D, 64), md.T, lu.transpose(1, 0, 2).reshape(D, 64)], axis=1))
    dgnb = np.ascontiguousarray(np.concatenate([ldg.transpose(0, 2, 1).reshape(64, D), bw], axis=0))
    misc = np.ascontiguousarray(np.concatenate(
        [np.asarray(L0_ffn_imp_center, dtype=np.float32),
         np.asarray(L0_ffn_imp_slope, dtype=np.float32)]).reshape(1, 16))
    ident = np.eye(128, dtype=bf)
    kk, jj = np.meshgrid(np.arange(128), np.arange(128), indexing="ij")
    causal = np.where(jj >= kk, 0.0, NEGBIG).astype(np.float32)

    if _NC_CACHE[0] is None:
        _NC_CACHE[0] = build_program()
    nc = _NC_CACHE[0]

    in_maps = []
    for c in range(NCORES):
        in_maps.append({
            "x_own": np.ascontiguousarray(x2d[c * TPC:(c + 1) * TPC]),
            "w_attnT": w_attnT,
            "b_att_st": b_att_st,
            "w_projT": w_projT,
            "b_proj_in": b_proj_in,
            "ffn_w": ffn_w,
            "dgnb": dgnb,
            "mean8": md,
            "debias8": dw,
            "misc_in": misc,
            "ident_in": ident,
            "causal_in": causal,
        })
    res = run_bass_kernel_spmd(nc, in_maps, core_ids=list(range(NCORES)))
    out = np.concatenate([res.results[c]["out_own"] for c in range(NCORES)], axis=0)
    return out.reshape(1, T, D)
